# version 24
# baseline (speedup 1.0000x reference)
import numpy as np
import ml_dtypes
from contextlib import ExitStack
import concourse.bass as bass
import concourse.mybir as mybir
from concourse.bass_utils import run_bass_kernel_spmd

F32 = mybir.dt.float32
BF16 = mybir.dt.bfloat16
ALU = mybir.AluOpType
AF = mybir.ActivationFunctionType
AX = mybir.AxisListType

D = 1024
CH = 128
IF = 2816
LN_EPS = 1e-5
MSQ = [0.0625, 0.5625, 1.5625, 3.0625, 6.25, 12.25, 25.0]
N_REC = 16
N_OWN = 16


class Sched:
    def __init__(self):
        self.record = True
        self.labels = {}
        self.counts = {}

    def inc(self, sem, label, amount=1):
        if self.record:
            v = self.counts.get(sem, 0) + amount
            self.counts[sem] = v
            assert (sem, label) not in self.labels, ("dup", sem, label)
            self.labels[(sem, label)] = v

    def val(self, sem, label):
        if self.record:
            return 1
        assert (sem, label) in self.labels, ("missing", sem, label)
        return self.labels[(sem, label)]


class FakeIns:
    def then_inc(self, *a, **k):
        return self


class FakeEng:
    def __getattr__(self, name):
        if name in ("wait_ge",):
            return lambda *a, **k: None
        return lambda *a, **k: FakeIns()


_LAST_SCHED = {}


def build_program(n_rec=N_REC, n_own=N_OWN):
    nc = bass.Bass("TRN2", target_bir_lowering=False)
    T_OWN, T_REC = n_own * CH, n_rec * CH

    dx = nc.dram_tensor("x_own", [T_OWN, D], F32, kind="ExternalInput").ap()
    dxr = nc.dram_tensor("x_rec", [T_REC, D], F32, kind="ExternalInput").ap()
    dgate = nc.dram_tensor("gatev", [128, 1], F32, kind="ExternalInput").ap()
    donehot = nc.dram_tensor("onehot", [128, 1], F32, kind="ExternalInput").ap()
    dident = nc.dram_tensor("ident", [128, 128], F32, kind="ExternalInput").ap()
    dwq = nc.dram_tensor("WqTp", [D, 512], F32, kind="ExternalInput").ap()
    dwk = nc.dram_tensor("WkTp", [D, 512], F32, kind="ExternalInput").ap()
    dwv = nc.dram_tensor("WvT", [D, D], F32, kind="ExternalInput").ap()
    dwo = nc.dram_tensor("WoT", [D, D], F32, kind="ExternalInput").ap()
    dwg = nc.dram_tensor("w_gate", [IF, D], F32, kind="ExternalInput").ap()
    dwu = nc.dram_tensor("w_up", [IF, D], F32, kind="ExternalInput").ap()
    dwd = nc.dram_tensor("w_down", [D, IF], F32, kind="ExternalInput").ap()
    dout = nc.dram_tensor("out", [T_OWN, D], F32, kind="ExternalOutput").ap()
    dgq = nc.dram_tensor("wgTq", [D, IF], BF16, kind="Internal").ap()
    duq = nc.dram_tensor("wuTq", [D, IF], BF16, kind="Internal").ap()
    ddq = nc.dram_tensor("wdTq", [IF, D], BF16, kind="Internal").ap()
    MATD = {"g": (dwg, dgq, D), "u": (dwu, duq, D), "d": (dwd, ddq, IF)}

    es = ExitStack()
    sch = Sched()
    with es:
        S_ = lambda n, sh, dt: es.enter_context(nc.sbuf_tensor(n, sh, dt))
        P_ = lambda n, sh, dt=F32: es.enter_context(nc.psum_tensor(n, sh, dt))

        identb = S_("identb", [128, 128], BF16)
        gate = S_("gate_s", [128, 1], F32)
        onehot = S_("onehot_s", [128, 1], F32)
        ones64 = S_("ones64", [1, 64], BF16)
        six = S_("six", [128, 1], F32)
        four = S_("four", [128, 1], F32)
        halfb = S_("halfb", [128, 1], F32)
        r6 = S_("r6", [128, 1], F32)
        wq_s = S_("wq_s", [128, 4096], BF16)
        wk_s = S_("wk_s", [128, 4096], BF16)
        wv_s = S_("wv_s", [128, 8192], BF16)
        wo_s = S_("wo_s", [128, 8192], BF16)
        wstage = S_("wstage", [128, 2048], F32)
        wqn = S_("wqn", [128, IF], BF16)
        wsp = S_("wsp", [128, 512], BF16)
        gu_buf = S_("gu_buf", [128, 4096], BF16)
        wd_buf = S_("wd_buf", [128, 5632], BF16)
        xt = S_("xt", [128, 2 * D], F32)
        bn1 = S_("bn1", [128, 12], F32)
        mv1 = S_("mv1", [128, 2], F32)
        veps1 = S_("veps1", [128, 1], F32)
        rstd1 = S_("rstd1", [128, 1], F32)
        nbias1 = S_("nbias1", [128, 1], F32)
        xn1 = S_("xn1", [128, D], BF16)
        xnf = S_("xnf", [128, D], BF16)
        qfp = S_("qfp", [128, 512], BF16)
        kfp = S_("kfp", [128, 512], BF16)
        ktm = S_("ktm", [128, 512], BF16)
        k32 = S_("k32", [128, 512], BF16)
        qtm = S_("qtm", [128, 512], BF16)
        kf = S_("kf", [128, 16 * 273], BF16)
        q2t = S_("q2t", [128, 2048], BF16)
        q2f = S_("q2f", [128, 2048], BF16)
        vga = S_("vga", [128, 16 * 65], BF16)
        att = S_("att", [128, 1024], BF16)
        psc = S_("psc", [128, 256], F32)
        sb1 = S_("sb1", [128, 260], F32)
        sb1b = S_("sb1b", [128, 260], BF16)
        sq_ = S_("sq_", [128, 2080], F32)
        sqb = S_("sqb", [128, 2080], BF16)
        ysb = S_("ysb", [128, 1024], F32)
        rden = S_("rden", [1, 1024], BF16)
        yf = S_("yf", [128, 1024], BF16)
        x1 = S_("x1", [128, 2 * D], F32)
        bn2 = S_("bn2", [128, 12], F32)
        mv2 = S_("mv2", [128, 2], F32)
        veps2 = S_("veps2", [128, 1], F32)
        rstd2 = S_("rstd2", [128, 1], F32)
        nbias2 = S_("nbias2", [128, 1], F32)
        xn2 = S_("xn2", [128, D], F32)
        xq2 = S_("xq2", [128, 2 * D], BF16)
        gsil = S_("gsil", [128, 512], F32)
        hq_t = S_("hq_t", [128, 2 * IF], BF16)
        outb = S_("outb", [128, 2 * D], F32)
        q_m = S_("q_m", [128, 128], F32)
        q_sh = S_("q_sh", [128, 128], F32)
        q_r = S_("q_r", [128, 128], F32)
        q_t = S_("q_t", [128, 2048], F32)
        q_u = S_("q_u", [128, 2048], F32)
        q_a = S_("q_a", [128, 2048], BF16)
        q_b = S_("q_b", [128, 2048], BF16)

        p_sc = P_("p_sc", [128, 512])
        p_y = P_("p_y", [128, 512])
        p_pa = P_("p_pa", [128, 512])
        p_ds = P_("p_ds", [128, 512])
        p_pb = P_("p_pb", [128, 512])
        p_fa = P_("p_fa", [128, 512])
        p_fb = P_("p_fb", [128, 512])
        p_tr = P_("p_tr", [128, 1024], BF16)

        sems = {n: es.enter_context(nc.semaphore(n)) for n in
                ["P", "V", "S", "G", "Dk", "Dw", "Dw0", "Dw1", "Dx0", "Dx1", "Do0", "Do1",
                 "Ws", "Wg00", "Wg01", "Wg10", "Wg11", "Wd"]}

        ALLC = [("r", c) for c in range(n_rec)] + [("o", c) for c in range(n_own)]
        TAG = {cc: f"{cc[0]}{cc[1]}" for cc in ALLC}

        WQ_UNITS = []
        for mat, nrt in (("d", 8), ("g", 22), ("u", 22)):
            r = 0
            while r < nrt:
                WQ_UNITS.append((mat, r, 1))
                r += 1
        early = ALLC[:n_rec + 1] if len(ALLC) > n_rec else ALLC
        WQ_AT = {}
        for i, u in enumerate(WQ_UNITS):
            WQ_AT.setdefault(early[i % len(early)], []).append(u)
        WQ_ORDER = []
        for cc in ALLC:
            WQ_ORDER.extend(WQ_AT.get(cc, []))
        WQ_PREV = {}
        for i, u in enumerate(WQ_ORDER):
            WQ_PREV[(u[0], u[1])] = WQ_ORDER[i - 1] if i >= 1 else None
        WQ_SLICES = []
        for (mat, r0, nr) in WQ_ORDER:
            W = MATD[mat][2]
            nsl = (W + 1023) // 1024
            for j in range(nr):
                for sl in range(nsl):
                    WQ_SLICES.append((mat, r0, j, sl))
        WQ_SLICE_IDX = {k: i for i, k in enumerate(WQ_SLICES)}
        WQ_LADDERS = {}
        WQ_LOADS = {}
        WQ_UNIT_LAST = {}
        _slot_reader = {0: None, 1: None}
        _next_slot = 0
        for cc in ALLC:
            lads = WQ_LADDERS.setdefault(cc, [])
            loads = WQ_LOADS.setdefault(cc, [])
            for (mat, r0, nr) in WQ_AT.get(cc, []):
                W = MATD[mat][2]
                nsl = (W + 1023) // 1024
                for j in range(nr):
                    sl = 0
                    while sl < nsl:
                        if mat == "d" and sl == 0:
                            segs = [(j, 0, 0, 1024, 0), (j, 1, 1024, 2048, 1)]
                            lbl = f"{mat}{r0}_{j}_0p"
                            sl = 2
                            _next_slot = 0
                        else:
                            w0 = sl * 1024
                            w1 = min(w0 + 1024, W)
                            segs = [(j, sl, w0, w1, _next_slot)]
                            lbl = f"{mat}{r0}_{j}_{sl}"
                            sl += 1
                            _next_slot ^= 1
                        for (jj, ssl, w0, w1, slot) in segs:
                            loads.append((mat, r0, jj, ssl, w0, w1, slot,
                                          _slot_reader[slot], f"wql_{mat}{r0}_{jj}_{ssl}"))
                        for (jj, ssl, w0, w1, slot) in segs:
                            _slot_reader[slot] = f"qt_{lbl}"
                        lads.append((mat, r0, lbl, segs))
                        WQ_UNIT_LAST[(mat, r0)] = lbl
        last_spill = {}
        for (mat, r0, nr) in WQ_UNITS:
            W = MATD[mat][2]
            nbat = (W // 128 + 3) // 4
            last_spill[mat] = f"{mat}{r0}_b{nbat - 1}"
        lmat, lr0, _ = WQ_ORDER[-1]
        lW = MATD[lmat][2]
        lnbat = (lW // 128 + 3) // 4
        lncgb = min(4, lW // 128 - (lnbat - 1) * 4)
        LAST_SPILL_ALL = f"wqd_{lmat}{lr0}_b{lnbat - 1}_{lncgb - 1}"

        WLOADS = []
        for nm, src, dst, cols in (("wq", dwq, wq_s, 512), ("wk", dwk, wk_s, 512),
                                   ("wv", dwv, wv_s, 1024), ("wo", dwo, wo_s, 1024)):
            npieces = 8 * cols // 1024
            for p in range(npieces):
                WLOADS.append((f"{nm}{p}", src, dst, cols, p))

        def SW(sp):
            return 512 if sp < 5 else 256

        def q_vec(vector, src, dst, W, lbl):
            ng = W // 16
            g3 = src.rearrange("p (g w) -> p g w", w=16)
            vector.tensor_reduce(q_m[:, 0:ng], g3, axis=AX.X, op=ALU.max, apply_absolute_value=True)
            vector.drain()
            vector.reciprocal(q_r[:, 0:ng], q_m[:, 0:ng])
            vector.tensor_scalar(out=q_sh[:, 0:ng], in0=q_m[:, 0:ng], scalar1=1.0 / 12.0, scalar2=None,
                                 op0=ALU.mult)
            vector.drain()
            rb = q_r[:, 0:ng].to_broadcast([128, ng, 16])
            shb = q_sh[:, 0:ng].to_broadcast([128, ng, 16])
            ins = vector.tensor_tensor(out=q_t[:, 0:W].rearrange("p (g w) -> p g w", w=16),
                                       in0=g3, in1=rb, op=ALU.mult)
            ins.then_inc(sems["V"], 1)
            sch.inc("V", f"qt_{lbl}")
            vector.wait_ge(sems["S"], sch.val("S", f"qu_{lbl}"))
            vector.tensor_scalar(out=q_a[:, 0:W], in0=q_u[:, 0:W], scalar1=MSQ[0], scalar2=None, op0=ALU.is_gt)
            vector.tensor_scalar(out=q_b[:, 0:W], in0=q_u[:, 0:W], scalar1=MSQ[2], scalar2=None, op0=ALU.is_gt)
            vector.tensor_scalar(out=q_t[:, 0:W], in0=q_u[:, 0:W], scalar1=MSQ[4], scalar2=None, op0=ALU.is_gt)
            vector.drain()
            vector.scalar_tensor_tensor(out=q_a[:, 0:W], in0=q_u[:, 0:W], scalar=MSQ[1], in1=q_a[:, 0:W],
                                        op0=ALU.is_gt, op1=ALU.add)
            vector.scalar_tensor_tensor(out=q_b[:, 0:W], in0=q_u[:, 0:W], scalar=MSQ[3], in1=q_b[:, 0:W],
                                        op0=ALU.is_gt, op1=ALU.add)
            vector.scalar_tensor_tensor(out=q_t[:, 0:W], in0=q_u[:, 0:W], scalar=MSQ[5], in1=q_t[:, 0:W],
                                        op0=ALU.is_gt, op1=ALU.add)
            vector.tensor_scalar(out=q_u[:, 0:W], in0=q_u[:, 0:W], scalar1=MSQ[6], scalar2=4.0,
                                 op0=ALU.is_gt, op1=ALU.mult)
            vector.drain()
            vector.tensor_tensor(out=q_a[:, 0:W], in0=q_a[:, 0:W], in1=q_b[:, 0:W], op=ALU.add)
            vector.scalar_tensor_tensor(out=q_t[:, 0:W], in0=q_t[:, 0:W], scalar=2.0, in1=q_u[:, 0:W],
                                        op0=ALU.mult, op1=ALU.add)
            vector.drain()
            vector.tensor_tensor(out=q_a[:, 0:W], in0=q_a[:, 0:W], in1=q_t[:, 0:W], op=ALU.add)
            vector.tensor_tensor(out=q_b[:, 0:W].rearrange("p (g w) -> p g w", w=16),
                                 in0=dst, in1=shb, op=ALU.mult)
            vector.drain()
            return vector.tensor_tensor(out=dst, in0=q_a[:, 0:W], in1=q_b[:, 0:W], op=ALU.mult)

        def q_sc(scalar, lbl, W, dst):
            scalar.wait_ge(sems["V"], sch.val("V", f"qt_{lbl}"))
            ins = scalar.activation(dst, q_t[:, 0:W], AF.Sign)
            ins.then_inc(sems["S"], 1)
            sch.inc("S", f"qg_{lbl}")
            ins = scalar.activation(q_u[:, 0:W], q_t[:, 0:W], AF.Square, scale=6.0)
            ins.then_inc(sems["S"], 1)
            sch.inc("S", f"qu_{lbl}")

        def wq_unit_slices(cc):
            for (mat, r0, nr) in WQ_AT.get(cc, []):
                W = MATD[mat][2]
                nsl = (W + 1023) // 1024
                for j in range(nr):
                    for sl in range(nsl):
                        w0 = sl * 1024
                        w1 = min(w0 + 1024, W)
                        yield mat, r0, nr, j, sl, w0, w1, W

        def wq_units_sc(scalar, cc, units=None, qsc_only=False, skip_qsc=()):
            for (mat, r0, nr) in (WQ_AT.get(cc, []) if units is None else units):
                W = MATD[mat][2]
                if (mat, r0) not in skip_qsc:
                    for (lmat, lr0, lbl, segs) in WQ_LADDERS.get(cc, []):
                        if (lmat, lr0) != (mat, r0):
                            continue
                        Wtot = sum(w1 - w0 for (j, sl, w0, w1, slot) in segs)
                        dst0 = segs[0][0] * W + segs[0][2]
                        q_sc(scalar, lbl, Wtot, wqn[:, dst0:dst0 + Wtot])
                if not qsc_only:
                    wq_spill_evacs_unit(scalar, cc, mat, r0, nr)

        def sync_fn(sync):
            sync.dma_start(out=wstage[:, 0:128], in_=dident[:]).then_inc(sems["Dk"], 16)
            sync.dma_start(out=gate[:], in_=dgate[:]).then_inc(sems["Dk"], 16)
            sync.dma_start(out=onehot[:], in_=donehot[:]).then_inc(sems["Dk"], 16)
            sch.inc("Dk", "konst", 48)
            for i, (lbl, src, dst, cols, p) in enumerate(WLOADS):
                if i >= 2:
                    sync.wait_ge(sems["V"], sch.val("V", "wc_" + WLOADS[i - 2][0]))
                elif i == 0:
                    sync.wait_ge(sems["V"], sch.val("V", "idb"))
                kpp = 1024 // cols
                ap = src[p * kpp * 128:(p + 1) * kpp * 128, :].rearrange("(k p2) c -> p2 k c", p2=128)
                dst2 = wstage[:, (i % 2) * 1024:(i % 2) * 1024 + 1024].rearrange("p (k c) -> p k c", c=cols)
                sync.dma_start(out=dst2, in_=ap).then_inc(sems["Dw%d" % (i % 2)], 16)
                sch.inc("Dw%d" % (i % 2), "wl_" + lbl, 16)
            wq_prev = "wc_" + WLOADS[-1][0]
            for ci, cc in enumerate(ALLC):
                ph, c = cc
                tag = TAG[cc]
                if ci >= 2:
                    p2h, p2c = ALLC[ci - 2]
                    if p2h == "o":
                        sync.wait_ge(sems["V"], sch.val("V", f"x1b_o{p2c}"))
                    else:
                        sync.wait_ge(sems["S"], sch.val("S", f"ln1_r{p2c}"))
                src = dxr if ph == "r" else dx
                dxsem = "Dx%d" % (ci % 2)
                sync.dma_start(out=xt[:, (ci % 2) * D:(ci % 2 + 1) * D],
                               in_=src[c * CH:(c + 1) * CH, :]).then_inc(sems[dxsem], 16)
                sch.inc(dxsem, f"x_{tag}", 16)
                for (mat, r0, j, sl, w0, w1, slot, gate_lbl, load_lbl) in WQ_LOADS.get(cc, []):
                    dsrc = MATD[mat][0]
                    sync.wait_ge(sems["V"], sch.val("V", gate_lbl if gate_lbl is not None else wq_prev))
                    sync.dma_start(out=wstage[:, slot * 1024:slot * 1024 + w1 - w0],
                                   in_=dsrc[(r0 + j) * 128:(r0 + j + 1) * 128, w0:w1]).then_inc(sems["Dw%d" % slot], 16)
                    sch.inc("Dw%d" % slot, load_lbl, 16)
                if ph != "o":
                    continue
                g, c2 = c // 2, c % 2
                if c2 == 1:
                    for sp in range(6):
                        sw = SW(sp)
                        for mi, dq in ((0, dgq), (1, duq)):
                            for hk in range(2):
                                wgsem = "Wg%d%d" % (hk, mi)
                                if g == 0:
                                    sync.wait_ge(sems["Ws"], sch.val("Ws", LAST_SPILL_ALL))
                                if mi == 0:
                                    if not (g == 0 and sp == 0):
                                        lab = f"up_{g}_{sp - 1}_1" if sp >= 1 else f"up_{g - 1}_5_1"
                                        sync.wait_ge(sems["P"], sch.val("P", lab))
                                else:
                                    sync.wait_ge(sems["P"], sch.val("P", f"gt_{g}_{2 * sp + 1}"))
                                ap2 = dq[hk * 512:(hk + 1) * 512, sp * 512:sp * 512 + sw].rearrange("(k p2) c -> p2 k c", p2=128)
                                dgu = gu_buf[:, hk * 2048:hk * 2048 + 4 * sw].rearrange("p (k c) -> p k c", c=sw)
                                sync.dma_start(out=dgu, in_=ap2).then_inc(sems[wgsem], 16)
                                sch.inc(wgsem, f"gu_{g}_{sp}_{mi}_{hk}", 16)
                    for nh in range(2):
                        for kb in range(2):
                            if g == 0:
                                sync.wait_ge(sems["Ws"], sch.val("Ws", LAST_SPILL_ALL))
                            if not (g == 0 and nh == 0 and kb == 0):
                                pnh, pkb, pg = (nh, kb - 1, g) if kb >= 1 else ((nh - 1, 1, g) if nh >= 1 else (1, 1, g - 1))
                                sync.wait_ge(sems["P"], sch.val("P", f"dnb_{pg}_{pnh}_{pkb}"))
                            kk0 = kb * 11
                            ap3 = ddq[kk0 * 128:(kk0 + 11) * 128, nh * 512:(nh + 1) * 512].rearrange("(k p2) c -> p2 k c", p2=128)
                            dwdb = wd_buf[:].rearrange("p (k c) -> p k c", c=512)
                            sync.dma_start(out=dwdb, in_=ap3).then_inc(sems["Wd"], 16)
                            sch.inc("Wd", f"wd_{g}_{nh}_{kb}", 16)
                    for c4 in range(2):
                        cgl = 2 * g + c4
                        sync.wait_ge(sems["V"], sch.val("V", f"od_{g}_{c4}_1"))
                        dosem = "Do%d" % c4
                        sync.dma_start(out=dout[cgl * CH:(cgl + 1) * CH, :],
                                       in_=outb[:, c4 * D:(c4 + 1) * D]).then_inc(sems[dosem], 16)
                        sch.inc(dosem, f"st_{cgl}", 16)
            sync.wait_ge(sems["Do0"], sch.val("Do0", f"st_{n_own - 2}"))
            sync.wait_ge(sems["Do1"], sch.val("Do1", f"st_{n_own - 1}"))

        def wq_units_vec(vector, cc, units=None):
            uset = None if units is None else {(m, r) for (m, r, n) in units}
            for (mat, r0, lbl, segs) in WQ_LADDERS.get(cc, []):
                if uset is not None and (mat, r0) not in uset:
                    continue
                W = MATD[mat][2]
                for (j, sl, w0, w1, slot) in segs:
                    dwsem = "Dw%d" % slot
                    vector.wait_ge(sems[dwsem], sch.val(dwsem, f"wql_{mat}{r0}_{j}_{sl}"))
                j0, sl0 = segs[0][0], segs[0][1]
                if j0 == 0 and sl0 == 0 and WQ_PREV[(mat, r0)] is not None:
                    pmat, pr0, pnr = WQ_PREV[(mat, r0)]
                    pW = MATD[pmat][2]
                    pnbat = (pW // 128 + 3) // 4
                    vector.wait_ge(sems["P"], sch.val("P", f"wqt_{pmat}{pr0}_b{pnbat - 1}"))
                Wtot = sum(w1 - w0 for (j, sl, w0, w1, slot) in segs)
                base = segs[0][4] * 1024
                dst0 = segs[0][0] * W + segs[0][2]
                ins = q_vec(vector, wstage[:, base:base + Wtot],
                            wqn[:, dst0:dst0 + Wtot], Wtot, lbl)
                ins.then_inc(sems["V"], 1)
                sch.inc("V", f"wqq_{lbl}")

        def vec_fn(vector):
            vector.memset(four[:], 4.0)
            vector.memset(halfb[:], 0.5)
            vector.memset(ones64[:], 1.0)
            vector.memset(vga[:, 64::65], 1.0)
            vector.memset(kf[:, 16::273], 1.0)
            vector.memset(p_ds[:], 0.0)
            vector.memset(p_sc[:], 0.0)
            vector.memset(sb1[:], 0.0)
            vector.memset(sq_[:], 0.0)
            vector.drain()
            vector.wait_ge(sems["Dk"], sch.val("Dk", "konst"))
            ins = vector.tensor_copy(out=identb[:], in_=wstage[:, 0:128])
            ins.then_inc(sems["V"], 1)
            sch.inc("V", "idb")
            for i, (lbl, src, dst, cols, p) in enumerate(WLOADS):
                dwsem = "Dw%d" % (i % 2)
                vector.wait_ge(sems[dwsem], sch.val(dwsem, "wl_" + lbl))
                ins = vector.tensor_copy(out=dst[:, p * 1024:(p + 1) * 1024],
                                         in_=wstage[:, (i % 2) * 1024:(i % 2) * 1024 + 1024])
                ins.then_inc(sems["V"], 1)
                sch.inc("V", "wc_" + lbl)
            prev_own = None
            for ci, cc in enumerate(ALLC):
                ph, c = cc
                tag = TAG[cc]
                xs = xt[:, (ci % 2) * D:(ci % 2 + 1) * D]
                dxsem = "Dx%d" % (ci % 2)
                vector.wait_ge(sems[dxsem], sch.val(dxsem, f"x_{tag}"))
                vector.bn_stats(bn1[:, 0:6], xs[:, 0:512])
                vector.bn_stats(bn1[:, 6:12], xs[:, 512:1024])
                vector.drain()
                vector.bn_aggr(mv1[:], bn1[:])
                vector.drain()
                if ci >= 1:
                    vector.wait_ge(sems["G"], sch.val("G", f"rs1_{TAG[ALLC[ci - 1]]}"))
                    vector.wait_ge(sems["S"], sch.val("S", f"ln1_{TAG[ALLC[ci - 1]]}"))
                ins = vector.tensor_scalar(out=veps1[:], in0=mv1[:, 1:2], scalar1=LN_EPS, scalar2=None, op0=ALU.add)
                ins.then_inc(sems["V"], 1)
                sch.inc("V", f"lv1_{tag}")
                vector.wait_ge(sems["G"], sch.val("G", f"rs1_{tag}"))
                ins = vector.scalar_tensor_tensor(out=nbias1[:], in0=mv1[:, 0:1], scalar=-1.0, in1=rstd1[:],
                                                  op0=ALU.mult, op1=ALU.mult)
                ins.then_inc(sems["V"], 1)
                sch.inc("V", f"nb1_{tag}")
                if ph == "o":
                    ktm_vv = ktm[:].rearrange("p (bs j) -> p bs j", j=32)[:, :, 0:16]
                    kf_hv = kf[:].rearrange("p (h d) -> p h d", h=16)
                    vector.wait_ge(sems["S"], sch.val("S", f"ktm_{tag}"))
                    vector.wait_ge(sems["P"], sch.val("P", f"supq_{TAG[ALLC[ci - 1]]}_15"))
                    vector.tensor_scalar(out=k32[:], in0=ktm[:], scalar1=1.0 / 32, scalar2=None, op0=ALU.mult)
                    vector.drain()
                    vector.tensor_scalar(out=kf_hv[:, :, 0:16], in0=ktm_vv, scalar1=0.25, scalar2=None, op0=ALU.mult)
                    k32_vv = k32[:].rearrange("p (bs j) -> p bs j", j=32)
                    for i in range(16):
                        c1 = k32_vv[:, :, i:i + 1].to_broadcast([128, 16, 16])
                        ins = vector.tensor_tensor(out=kf_hv[:, :, 17 + 16 * i:33 + 16 * i],
                                                   in0=ktm_vv, in1=c1, op=ALU.mult)
                    ins.then_inc(sems["V"], 1)
                    sch.inc("V", f"kf_{tag}")
                    qtm_vv = qtm[:].rearrange("p (bs j) -> p bs j", j=32)[:, :, 0:16]
                    qtm_bv = qtm[:].rearrange("p (bs j) -> p bs j", j=32)
                    q2_hv = q2t[:].rearrange("p (h d) -> p h d", h=8)
                    for hh in range(2):
                        vector.wait_ge(sems["S"], sch.val("S", f"qtm_{tag}"))
                        if hh == 0:
                            if prev_own is not None:
                                vector.wait_ge(sems["P"], sch.val("P", f"q2t_{prev_own}_3"))
                        else:
                            vector.wait_ge(sems["P"], sch.val("P", f"q2t_{tag}_1"))
                        for i in range(16):
                            c1 = qtm_bv[:, hh * 8:(hh + 1) * 8, i:i + 1].to_broadcast([128, 8, 16])
                            ins = vector.tensor_tensor(out=q2_hv[:, :, 16 * i:16 * i + 16],
                                                       in0=qtm_vv[:, hh * 8:(hh + 1) * 8, :], in1=c1, op=ALU.mult)
                        ins.then_inc(sems["V"], 1)
                        sch.inc("V", f"q2b_{tag}_{hh}")

                def state_adds():
                    for grp in range(4):
                        vector.wait_ge(sems["P"], sch.val("P", f"supb_{tag}_{grp}"))
                        ins = vector.tensor_tensor(out=sb1[:, grp * 65:(grp + 1) * 65],
                                                   in0=sb1[:, grp * 65:(grp + 1) * 65],
                                                   in1=DBL(grp), op=ALU.add)
                        ins.then_inc(sems["V"], 1)
                        sch.inc("V", f"adb_{tag}_{grp}")
                    for h in range(16):
                        vector.wait_ge(sems["P"], sch.val("P", f"supq_{tag}_{h}"))
                        ins = vector.tensor_tensor(out=sq_[:, h * 130:(h + 1) * 130],
                                                   in0=sq_[:, h * 130:(h + 1) * 130],
                                                   in1=DQL(h), op=ALU.add)
                        ins.then_inc(sems["V"], 1)
                        sch.inc("V", f"adq_{tag}_{h}")

                if ph == "r":
                    units = WQ_AT.get(cc, [])
                    wq_units_vec(vector, cc, units=units[:1])
                    state_adds()
                    wq_units_vec(vector, cc, units=units[1:])
                    if ci == n_rec - 1:
                        vector.drain()
                        vector.tensor_scalar(out=sb1[:], in0=sb1[:], scalar1=gate[:], scalar2=None, op0=ALU.mult)
                        ins = vector.tensor_scalar(out=sq_[:], in0=sq_[:], scalar1=gate[:], scalar2=None, op0=ALU.mult)
                        ins.then_inc(sems["V"], 1)
                        sch.inc("V", "gated")
                    continue
                for grp in range(4):
                    vector.wait_ge(sems["S"], sch.val("S", f"yev_{tag}_{grp}"))
                    if grp >= 2:
                        vector.wait_ge(sems["P"], sch.val("P", f"rep_{tag}_{(grp - 2) * 4 + 3}"))
                    elif prev_own is not None:
                        vector.wait_ge(sems["P"], sch.val("P", f"rep_{prev_own}_{(grp + 2) * 4 + 3}"))
                    with nc.allow_low_precision(reason="bf16 recip-den feeds a bf16 matmul broadcast; tolerance budgeted"):
                        ins = vector.reciprocal(rden[0:1, (grp % 2) * 512:(grp % 2) * 512 + 512],
                                                ysb[64:65, (grp % 2) * 512:(grp % 2) * 512 + 512])
                    ins.then_inc(sems["V"], 1)
                    sch.inc("V", f"rd_{tag}_{grp}")
                    for hh in range(4):
                        h = grp * 4 + hh
                        if h == 0 and prev_own is not None:
                            vector.wait_ge(sems["P"], sch.val("P", f"ao_{prev_own}_1"))
                        vector.wait_ge(sems["P"], sch.val("P", f"rep_{tag}_{h}"))
                        ins = vector.tensor_tensor(
                            out=yf[64 * (h % 2):64 * (h % 2) + 64, (h // 2) * 128:(h // 2 + 1) * 128],
                            in0=ysb[0:64, (grp % 2) * 512 + hh * 128:(grp % 2) * 512 + hh * 128 + 128],
                            in1=RSL(h), op=ALU.mult)
                        ins.then_inc(sems["V"], 1)
                        sch.inc("V", f"yd_{tag}_{h}")
                state_adds()
                for half in range(2):
                    vector.wait_ge(sems["P"], sch.val("P", f"ao_{tag}_{half}"))
                    if half == 0 and c >= 2:
                        vector.wait_ge(sems["S"], sch.val("S", f"ln2_o{c - 2}"))
                    ins = vector.tensor_tensor(out=x1[:, (c % 2) * D + half * 512:(c % 2) * D + half * 512 + 512],
                                               in0=xs[:, half * 512:half * 512 + 512],
                                               in1=(p_pa if half == 0 else p_pb)[:, 0:512], op=ALU.add)
                    ins.then_inc(sems["V"], 1)
                    sch.inc("V", f"x1_{tag}_0" if half == 0 else f"x1b_{tag}")
                xv = x1[:, (c % 2) * D:(c % 2 + 1) * D]
                vector.drain()
                vector.bn_stats(bn2[:, 0:6], xv[:, 0:512])
                vector.bn_stats(bn2[:, 6:12], xv[:, 512:1024])
                vector.drain()
                vector.bn_aggr(mv2[:], bn2[:])
                vector.drain()
                if prev_own is not None:
                    vector.wait_ge(sems["G"], sch.val("G", f"rs2_{prev_own}"))
                    vector.wait_ge(sems["S"], sch.val("S", f"ln2_{prev_own}"))
                ins = vector.tensor_scalar(out=veps2[:], in0=mv2[:, 1:2], scalar1=LN_EPS, scalar2=None, op0=ALU.add)
                ins.then_inc(sems["V"], 1)
                sch.inc("V", f"lv2_{tag}")
                vector.wait_ge(sems["G"], sch.val("G", f"rs2_{tag}"))
                ins = vector.scalar_tensor_tensor(out=nbias2[:], in0=mv2[:, 0:1], scalar=-1.0, in1=rstd2[:],
                                                  op0=ALU.mult, op1=ALU.mult)
                ins.then_inc(sems["V"], 1)
                sch.inc("V", f"nb2_{tag}")
                vector.wait_ge(sems["S"], sch.val("S", f"ln2_{tag}"))
                if c >= 2:
                    vector.wait_ge(sems["P"], sch.val("P", f"up_{(c - 2) // 2}_5_{c % 2}"))
                ins = q_vec(vector, xn2[:], xq2[:, (c % 2) * D:(c % 2 + 1) * D], D, f"xq_{tag}")
                ins.then_inc(sems["V"], 1)
                sch.inc("V", f"xq_{tag}")
                g, c2 = c // 2, c % 2
                if c2 == 1:
                    for pq in range(3):
                        sp0, sp1 = 2 * pq, 2 * pq + 1
                        w0, w1 = SW(sp0), SW(sp1)
                        for sp in (sp0, sp1):
                            sw = SW(sp)
                            for c4 in range(2):
                                t = sp * 2 + c4
                                vector.wait_ge(sems["S"], sch.val("S", f"sil_{g}_{t}"))
                                vector.wait_ge(sems["P"], sch.val("P", f"up_{g}_{sp}_{c4}"))
                                off = c4 * (w0 + w1) + (w0 if sp == sp1 else 0)
                                ins = vector.tensor_tensor(out=q_t[:, off:off + sw],
                                                           in0=gsil[:, 0:sw], in1=p_fb[:, 0:sw], op=ALU.mult)
                                ins.then_inc(sems["V"], 1)
                                sch.inc("V", f"hm_{g}_{t}")
                        vector.drain()
                        if g >= 1 and pq == 0:
                            vector.wait_ge(sems["P"], sch.val("P", f"dnm_{g - 1}_0_1"))
                            vector.wait_ge(sems["P"], sch.val("P", f"dnm_{g - 1}_1_1"))
                        f0 = sp0 * 512
                        dstap = hq_t[:].rearrange("p (c4 f) -> p c4 f", c4=2)[:, :, f0:f0 + w0 + w1]
                        ins = q_vec(vector, q_t[:, 0:2 * (w0 + w1)], dstap, 2 * (w0 + w1), f"hq_{g}_{pq}")
                        ins.then_inc(sems["V"], 1)
                        sch.inc("V", f"hq_{g}_{pq}")
                    for nh in range(2):
                        for c4 in range(2):
                            cgl = 2 * g + c4
                            vector.wait_ge(sems["P"], sch.val("P", f"dnm_{g}_{c4}_{nh}"))
                            if nh == 0 and g >= 1:
                                dosem = "Do%d" % c4
                                vector.wait_ge(sems[dosem], sch.val(dosem, f"st_{2 * (g - 1) + c4}"))
                            ins = vector.tensor_tensor(
                                out=outb[:, c4 * D + nh * 512:c4 * D + nh * 512 + 512],
                                in0=x1[:, (cgl % 2) * D + nh * 512:(cgl % 2) * D + nh * 512 + 512],
                                in1=(p_fa if c4 == 0 else p_fb)[:, 0:512], op=ALU.add)
                            ins.then_inc(sems["V"], 1)
                            sch.inc("V", f"od_{g}_{c4}_{nh}")
                wq_units_vec(vector, cc)
                prev_own = tag

        def gp_fn(gpsimd):
            ktm_v = ktm[:].rearrange("p (bs j) -> p bs j", j=32)[:, :, 0:16]
            qtm_v = qtm[:].rearrange("p (bs j) -> p bs j", j=32)[:, :, 0:16]
            kf_h = kf[:].rearrange("p (h d) -> p h d", h=16)
            q2_h = q2t[:].rearrange("p (h d) -> p h d", h=8)
            prev_own = None
            prev = None
            for ci, cc in enumerate(ALLC):
                ph, c = cc
                tag = TAG[cc]
                gpsimd.wait_ge(sems["V"], sch.val("V", f"lv1_{tag}"))
                if prev is not None:
                    gpsimd.wait_ge(sems["S"], sch.val("S", f"ln1_{prev}"))
                gpsimd.tensor_tensor(out=rstd1[:], in0=veps1[:], in1=nhalf_tile(), op=ALU.pow)
                gpsimd.drain().then_inc(sems["G"], 1)
                sch.inc("G", f"rs1_{tag}")
                if ph == "r":
                    gpsimd.wait_ge(sems["S"], sch.val("S", f"ktm_{tag}"))
                    if prev is not None and prev.startswith("r"):
                        gpsimd.wait_ge(sems["P"], sch.val("P", f"supq_{prev}_15"))
                    gpsimd.tensor_scalar(out=k32[:], in0=ktm[:], scalar1=1.0 / 32, scalar2=None, op0=ALU.mult)
                    gpsimd.drain()
                    gpsimd.tensor_scalar(out=kf_h[:, :, 0:16], in0=ktm_v, scalar1=0.25, scalar2=None, op0=ALU.mult)
                    k32_v = k32[:].rearrange("p (bs j) -> p bs j", j=32)
                    for i in range(16):
                        c1 = k32_v[:, :, i:i + 1].to_broadcast([128, 16, 16])
                        gpsimd.tensor_tensor(out=kf_h[:, :, 17 + 16 * i:33 + 16 * i], in0=ktm_v, in1=c1, op=ALU.mult)
                    gpsimd.drain().then_inc(sems["G"], 1)
                    sch.inc("G", f"kf_{tag}")
                    prev = tag
                    continue
                for p2 in range(8):
                    h0, h1 = 2 * p2, 2 * p2 + 1
                    gpsimd.wait_ge(sems["S"], sch.val("S", f"psq_{tag}_{h1}"))
                    if h0 < 8:
                        if prev_own is not None:
                            gpsimd.wait_ge(sems["P"], sch.val("P", f"ymm_{prev_own}_{(h1 + 8) // 4}"))
                    else:
                        gpsimd.wait_ge(sems["P"], sch.val("P", f"ymm_{tag}_{(h1 - 8) // 4}"))
                    for h in (h0, h1):
                        gpsimd.affine_select(out=att[:, (h % 8) * 128:(h % 8) * 128 + 128],
                                             in_=psc[:, (h % 2) * 128:(h % 2) * 128 + 128],
                                             pattern=[[1, 128]], compare_op=ALU.is_ge, fill=0.0,
                                             base=0, channel_multiplier=-1)
                    gpsimd.drain().then_inc(sems["G"], 1)
                    sch.inc("G", f"attp_{tag}_{p2}")
                gpsimd.wait_ge(sems["V"], sch.val("V", f"lv2_{tag}"))
                if prev_own is not None:
                    gpsimd.wait_ge(sems["S"], sch.val("S", f"ln2_{prev_own}"))
                gpsimd.tensor_tensor(out=rstd2[:], in0=veps2[:], in1=nhalf_tile(), op=ALU.pow)
                gpsimd.drain().then_inc(sems["G"], 1)
                sch.inc("G", f"rs2_{tag}")
                prev = tag
                prev_own = tag

        nhalf = S_("nhalf", [128, 1], F32)

        def nhalf_tile():
            return nhalf[:]

        def wq_spill_evacs_unit(scalar, cc, mat, r0, nr):
            if True:
                W = MATD[mat][2]
                ncg = W // 128
                nbat = (ncg + 3) // 4
                for bt in range(nbat):
                    ncg_b = min(4, ncg - bt * 4)
                    scalar.wait_ge(sems["P"], sch.val("P", f"wqt_{mat}{r0}_b{bt}"))
                    if bt >= 1:
                        pn = min(4, ncg - (bt - 1) * 4)
                        scalar.wait_ge(sems["Ws"], sch.val("Ws", f"wqd_{mat}{r0}_b{bt - 1}_{pn - 1}"))
                    elif WQ_PREV[(mat, r0)] is not None:
                        pmat, pr0, pnr = WQ_PREV[(mat, r0)]
                        pW = MATD[pmat][2]
                        pncg = pW // 128
                        pnbat = (pncg + 3) // 4
                        pn = min(4, pncg - (pnbat - 1) * 4)
                        scalar.wait_ge(sems["Ws"], sch.val("Ws", f"wqd_{pmat}{pr0}_b{pnbat - 1}_{pn - 1}"))
                    ins = scalar.copy(out=wsp[:, 0:ncg_b * nr * 128], in_=p_tr[:, 0:ncg_b * nr * 128])
                    ins.then_inc(sems["S"], 1)
                    sch.inc("S", f"wsp_{mat}{r0}_b{bt}")
                    scalar.wait_ge(sems["S"], sch.val("S", f"wsp_{mat}{r0}_b{bt}"))
                    ddst = MATD[mat][1]
                    for cgi in range(ncg_b):
                        cg = bt * 4 + cgi
                        scalar.dma_start(out=ddst[cg * 128:(cg + 1) * 128, r0 * 128:(r0 + nr) * 128],
                                         in_=wsp[:, cgi * nr * 128:(cgi + 1) * nr * 128]).then_inc(sems["Ws"], 16)
                        sch.inc("Ws", f"wqd_{mat}{r0}_b{bt}_{cgi}", 16)

        def sc_fn(scalar):
            prev = None
            prev_own = None
            for ci, cc in enumerate(ALLC):
                ph, c = cc
                tag = TAG[cc]
                xs = xt[:, (ci % 2) * D:(ci % 2 + 1) * D]
                if ph == "r":
                    wq_units_sc(scalar, cc, units=WQ_AT.get(cc, [])[:1], qsc_only=True)
                scalar.wait_ge(sems["V"], sch.val("V", f"nb1_{tag}"))
                if prev is not None:
                    scalar.wait_ge(sems["P"], sch.val("P", f"xnt_{prev}"))
                ins = scalar.activation(xn1[:], xs, AF.Identity, scale=rstd1[:], bias=nbias1[:])
                ins.then_inc(sems["S"], 1)
                sch.inc("S", f"ln1_{tag}")
                scalar.wait_ge(sems["P"], sch.val("P", f"xnt_{tag}"))
                wlab = f"qproj_{prev}" if (prev is not None and prev.startswith("o")) else (f"vproj_{prev}_1" if prev is not None else None)
                if wlab is not None:
                    scalar.wait_ge(sems["P"], sch.val("P", wlab))
                ins = scalar.copy(out=xnf[:], in_=p_tr[:, 0:1024])
                ins.then_inc(sems["S"], 1)
                sch.inc("S", f"xnf_{tag}")
                scalar.wait_ge(sems["P"], sch.val("P", f"kproj_{tag}"))
                if prev is not None:
                    lab = f"sc_{prev}_15" if prev.startswith("o") else f"ktr_{prev}"
                    scalar.wait_ge(sems["P"], sch.val("P", lab))
                ins = scalar.copy(out=kfp[:], in_=p_pb[:, 0:512])
                ins.then_inc(sems["S"], 1)
                sch.inc("S", f"kfp_{tag}")
                for half in range(2):
                    scalar.wait_ge(sems["P"], sch.val("P", f"vproj_{tag}_{half}"))
                    if prev is not None and half == 0:
                        scalar.wait_ge(sems["P"], sch.val("P", f"supq_{prev}_15"))
                    dst = vga[:].rearrange("p (h e) -> p h e", h=16)[:, half * 8:(half + 1) * 8, 0:64]
                    ins = scalar.copy(out=dst, in_=(p_pa if half == 0 else p_pb)[:, 0:512])
                    ins.then_inc(sems["S"], 1)
                    sch.inc("S", f"vga_{tag}_{half}")
                scalar.wait_ge(sems["P"], sch.val("P", f"ktr_{tag}"))
                if prev is not None:
                    ksem = "G" if prev.startswith("r") else "V"
                    scalar.wait_ge(sems[ksem], sch.val(ksem, f"kf_{prev}"))
                ins = scalar.copy(out=ktm[:], in_=p_tr[:, 0:512])
                ins.then_inc(sems["S"], 1)
                sch.inc("S", f"ktm_{tag}")
                if ph == "r":
                    wq_units_sc(scalar, cc, skip_qsc={(m, r) for (m, r, n) in WQ_AT.get(cc, [])[:1]})
                    if ci == n_rec - 1:
                        scalar.wait_ge(sems["V"], sch.val("V", "gated"))
                        scalar.copy(out=sb1b[:], in_=sb1[:])
                        ins = scalar.copy(out=sqb[:], in_=sq_[:])
                        ins.then_inc(sems["S"], 1)
                        sch.inc("S", "mir_boot")
                    prev = tag
                    continue
                scalar.wait_ge(sems["P"], sch.val("P", f"qproj_{tag}"))
                if prev_own is not None:
                    scalar.wait_ge(sems["P"], sch.val("P", f"ymm_{prev_own}_3"))
                ins = scalar.activation(qfp[:], p_pa[:, 0:512], AF.Identity, bias=onehot[:], scale=1.0)
                ins.then_inc(sems["S"], 1)
                sch.inc("S", f"qfp_{tag}")
                scalar.wait_ge(sems["P"], sch.val("P", f"qtr_{tag}"))
                if prev_own is not None:
                    scalar.wait_ge(sems["V"], sch.val("V", f"q2b_{prev_own}_1"))
                ins = scalar.copy(out=qtm[:], in_=p_tr[:, 512:1024])
                ins.then_inc(sems["S"], 1)
                sch.inc("S", f"qtm_{tag}")
                for half2 in range(2):
                    for gq in (2 * half2, 2 * half2 + 1):
                        scalar.wait_ge(sems["P"], sch.val("P", f"q2t_{tag}_{gq}"))
                        if gq >= 2:
                            scalar.wait_ge(sems["P"], sch.val("P", f"ymm_{tag}_{gq - 2}"))
                        elif prev_own is not None:
                            scalar.wait_ge(sems["P"], sch.val("P", f"ymm_{prev_own}_{gq + 2}"))
                        ins = scalar.copy(out=q2f[:, (gq % 2) * 1024:(gq % 2 + 1) * 1024], in_=p_tr[:, 0:1024])
                        ins.then_inc(sems["S"], 1)
                        sch.inc("S", f"q2f_{tag}_{gq}")
                    for h in range(8 * half2, 8 * half2 + 8):
                        scalar.wait_ge(sems["P"], sch.val("P", f"sc_{tag}_{h}"))
                        if h >= 2:
                            scalar.wait_ge(sems["G"], sch.val("G", f"attp_{tag}_{(h - 2) // 2}"))
                        elif prev_own is not None:
                            scalar.wait_ge(sems["G"], sch.val("G", f"attp_{prev_own}_7"))
                        ins = scalar.activation(psc[:, (h % 2) * 128:(h % 2) * 128 + 128],
                                                SCB(h),
                                                AF.Square, bias=four[:], scale=1.0)
                        ins.then_inc(sems["S"], 1)
                        sch.inc("S", f"psqa_{tag}_{h}")
                        scalar.wait_ge(sems["S"], sch.val("S", f"psqa_{tag}_{h}"))
                        ins = scalar.activation(psc[:, (h % 2) * 128:(h % 2) * 128 + 128],
                                                psc[:, (h % 2) * 128:(h % 2) * 128 + 128],
                                                AF.Identity, bias=halfb[:], scale=1.0 / 32.0)
                        ins.then_inc(sems["S"], 1)
                        sch.inc("S", f"psq_{tag}_{h}")
                    for grp in (2 * half2, 2 * half2 + 1):
                        scalar.wait_ge(sems["P"], sch.val("P", f"ymm_{tag}_{grp}"))
                        if grp >= 2:
                            scalar.wait_ge(sems["V"], sch.val("V", f"yd_{tag}_{(grp - 2) * 4 + 3}"))
                        elif prev_own is not None:
                            scalar.wait_ge(sems["V"], sch.val("V", f"yd_{prev_own}_{(grp + 2) * 4 + 3}"))
                        ins = scalar.copy(out=ysb[0:65, (grp % 2) * 512:(grp % 2) * 512 + 512],
                                          in_=YBANK(grp))
                        ins.then_inc(sems["S"], 1)
                        sch.inc("S", f"yev_{tag}_{grp}")
                scalar.wait_ge(sems["V"], sch.val("V", f"adq_{tag}_15"))
                scalar.copy(out=sb1b[:], in_=sb1[:])
                ins = scalar.copy(out=sqb[:], in_=sq_[:])
                ins.then_inc(sems["S"], 1)
                sch.inc("S", f"mir_{tag}")
                scalar.wait_ge(sems["V"], sch.val("V", f"nb2_{tag}"))
                if c >= 1:
                    scalar.wait_ge(sems["V"], sch.val("V", f"xq_o{c - 1}"))
                xv = x1[:, (c % 2) * D:(c % 2 + 1) * D]
                ins = scalar.activation(xn2[:], xv, AF.Identity, scale=rstd2[:], bias=nbias2[:])
                ins.then_inc(sems["S"], 1)
                sch.inc("S", f"ln2_{tag}")
                q_sc(scalar, f"xq_{tag}", D, xq2[:, (c % 2) * D:(c % 2 + 1) * D])
                scalar.wait_ge(sems["P"], sch.val("P", f"xqt_{tag}"))
                ins = scalar.copy(out=xq2[:, (c % 2) * D:(c % 2 + 1) * D], in_=p_tr[:, 0:1024])
                ins.then_inc(sems["S"], 1)
                sch.inc("S", f"xqf_{tag}")
                g, c2 = c // 2, c % 2
                if c2 == 1:
                    for pq in range(3):
                        sp0, sp1 = 2 * pq, 2 * pq + 1
                        w0, w1 = SW(sp0), SW(sp1)
                        for sp in (sp0, sp1):
                            for c4 in range(2):
                                t = sp * 2 + c4
                                scalar.wait_ge(sems["P"], sch.val("P", f"gt_{g}_{t}"))
                                if t >= 1:
                                    scalar.wait_ge(sems["V"], sch.val("V", f"hm_{g}_{t - 1}"))
                                elif g >= 1:
                                    scalar.wait_ge(sems["V"], sch.val("V", f"hm_{g - 1}_11"))
                                sw = SW(sp)
                                ins = scalar.activation(gsil[:, 0:sw], p_fa[:, 0:sw], AF.Silu)
                                ins.then_inc(sems["S"], 1)
                                sch.inc("S", f"sil_{g}_{t}")
                        hdst = hq_t[:].rearrange("p (c4 f) -> p c4 f", c4=2)[:, :, sp0 * 512:sp0 * 512 + w0 + w1]
                        q_sc(scalar, f"hq_{g}_{pq}", 2 * (w0 + w1), hdst)
                    for c4 in range(2):
                        for bq in range(3):
                            w = 1024 if bq < 2 else 768
                            scalar.wait_ge(sems["P"], sch.val("P", f"hqt_{g}_{c4}_{bq}"))
                            ins = scalar.copy(out=hq_t[:, c4 * IF + bq * 1024:c4 * IF + bq * 1024 + w],
                                              in_=p_tr[:, 0:w])
                            ins.then_inc(sems["S"], 1)
                            sch.inc("S", f"hqf_{g}_{c4}_{bq}")
                wq_units_sc(scalar, cc)
                prev = tag
                prev_own = tag

        def SCB(h):
            return (p_pa if h % 2 == 0 else p_pb)[:, 0:128]

        def YSL(h):
            bk = p_y if (h // 4) % 2 == 0 else p_sc
            return bk[0:65, (h % 4) * 128:(h % 4) * 128 + 128]

        def YBANK(grp):
            bk = p_y if grp % 2 == 0 else p_sc
            return bk[0:65, 0:512]

        def RSL(h):
            return (p_pa if h % 2 == 0 else p_pb)[0:64, 128:256]

        def DBL(grp):
            return p_ds[:, 0:65] if grp % 2 == 0 else p_sc[:, 256:321]

        def DQL(h):
            return p_ds[:, 65:195] if h % 2 == 0 else p_sc[:, 321:451]

        def last_tr_label(cc):
            ph, c = cc
            tag = TAG[cc]
            units = WQ_AT.get(cc, [])
            if units:
                mat, r0, nr = units[-1]
                W = MATD[mat][2]
                nbat = (W // 128 + 3) // 4
                return f"wsp_{mat}{r0}_b{nbat - 1}"
            if ph == "o" and c % 2 == 1:
                return f"hqf_{c // 2}_1_2"
            if ph == "o":
                return f"xqf_{tag}"
            return f"ktm_{tag}"

        def pre_wq_tr_label(cc):
            ph, c = cc
            tag = TAG[cc]
            if ph == "o" and c % 2 == 1:
                return f"hqf_{c // 2}_1_2"
            if ph == "o":
                return f"xqf_{tag}"
            return f"ktm_{tag}"

        def pinc(ins, label):
            ins.then_inc(sems["P"], 1)
            sch.inc("P", label)

        def pe_fn(tensor):
            tensor.wait_ge(sems["V"], sch.val("V", "wc_" + WLOADS[-1][0]))
            prev = None
            prev_own = None
            for ci, cc in enumerate(ALLC):
                ph, c = cc
                tag = TAG[cc]
                tensor.wait_ge(sems["S"], sch.val("S", f"ln1_{tag}"))
                if ci >= 1:
                    tensor.wait_ge(sems["S"], sch.val("S", last_tr_label(ALLC[ci - 1])))
                for t8 in range(8):
                    ins = tensor.transpose(p_tr[:, t8 * 128:(t8 + 1) * 128], xn1[:, t8 * 128:(t8 + 1) * 128], identb[:])
                pinc(ins, f"xnt_{tag}")
                tensor.wait_ge(sems["S"], sch.val("S", f"xnf_{tag}"))
                if prev is not None:
                    lab = f"x1b_{prev}" if prev.startswith("o") else None
                    if lab:
                        tensor.wait_ge(sems["V"], sch.val("V", lab))
                    else:
                        tensor.wait_ge(sems["S"], sch.val("S", f"vga_{prev}_1"))
                for b in range(4):
                    for k in range(8):
                        ins = tensor.matmul(p_pb[:, b * 128:(b + 1) * 128],
                                            lhsT=wk_s[:, k * 512 + b * 128:k * 512 + (b + 1) * 128],
                                            rhs=xnf[:, k * 128:(k + 1) * 128], start=(k == 0), stop=(k == 7))
                pinc(ins, f"kproj_{tag}")
                tensor.wait_ge(sems["S"], sch.val("S", f"kfp_{tag}"))
                if prev is not None:
                    lab = f"x1_{prev}_0" if prev.startswith("o") else None
                    if lab:
                        tensor.wait_ge(sems["V"], sch.val("V", lab))
                    else:
                        tensor.wait_ge(sems["S"], sch.val("S", f"vga_{prev}_0"))
                for half in range(2):
                    bank = p_pa if half == 0 else p_pb
                    for k in range(8):
                        ins = tensor.matmul(bank[:, 0:512], lhsT=xnf[:, k * 128:(k + 1) * 128],
                                            rhs=wv_s[:, k * 1024 + half * 512:k * 1024 + half * 512 + 512],
                                            start=(k == 0), stop=(k == 7))
                    pinc(ins, f"vproj_{tag}_{half}")
                tensor.wait_ge(sems["S"], sch.val("S", f"xnf_{tag}"))
                for b in range(4):
                    ins = tensor.transpose(p_tr[:, b * 128:(b + 1) * 128], kfp[:, b * 128:(b + 1) * 128], identb[:])
                pinc(ins, f"ktr_{tag}")

                def sup():
                    ksem = "G" if ph == "r" else "V"
                    tensor.wait_ge(sems[ksem], sch.val(ksem, f"kf_{tag}"))
                    tensor.wait_ge(sems["S"], sch.val("S", f"vga_{tag}_1"))
                    if ph == "o":
                        tensor.wait_ge(sems["V"], sch.val("V", f"yd_{tag}_15"))
                    for grp in range(4):
                        if grp >= 2:
                            tensor.wait_ge(sems["V"], sch.val("V", f"adb_{tag}_{grp - 2}"))
                        elif prev is not None:
                            tensor.wait_ge(sems["V"], sch.val("V", f"adb_{prev}_{grp + 2}"))
                        dbl = DBL(grp)
                        for s4 in range(4):
                            h = grp * 4 + s4
                            ins = tensor.matmul(dbl[32 * s4:32 * s4 + 17, 0:65],
                                                lhsT=kf[:, h * 273:h * 273 + 17],
                                                rhs=vga[:, h * 65:(h + 1) * 65], start=True, stop=True,
                                                tile_position=(0, 32 * s4))
                        pinc(ins, f"supb_{tag}_{grp}")
                    for h in range(16):
                        if h >= 2:
                            tensor.wait_ge(sems["V"], sch.val("V", f"adq_{tag}_{h - 2}"))
                        else:
                            tensor.wait_ge(sems["V"], sch.val("V", f"adb_{tag}_{h + 2}"))
                            if prev is not None:
                                tensor.wait_ge(sems["V"], sch.val("V", f"adq_{prev}_{h + 14}"))
                        dql = DQL(h)
                        for t2 in range(2):
                            ins = tensor.matmul(dql[:, t2 * 65:(t2 + 1) * 65],
                                                lhsT=kf[:, h * 273 + 17 + t2 * 128:h * 273 + 17 + (t2 + 1) * 128],
                                                rhs=vga[:, h * 65:(h + 1) * 65], start=True, stop=True)
                        pinc(ins, f"supq_{tag}_{h}")

                if ph == "r":
                    sup()
                    wq_transposes(tensor, cc)
                    prev = tag
                    continue
                tensor.wait_ge(sems["S"], sch.val("S", f"vga_{tag}_0"))
                for b in range(4):
                    for k in range(8):
                        ins = tensor.matmul(p_pa[:, b * 128:(b + 1) * 128],
                                            lhsT=wq_s[:, k * 512 + b * 128:k * 512 + (b + 1) * 128],
                                            rhs=xnf[:, k * 128:(k + 1) * 128], start=(k == 0), stop=(k == 7))
                pinc(ins, f"qproj_{tag}")
                tensor.wait_ge(sems["S"], sch.val("S", f"qfp_{tag}"))
                tensor.wait_ge(sems["S"], sch.val("S", f"ktm_{tag}"))
                for b in range(4):
                    ins = tensor.transpose(p_tr[:, 512 + b * 128:512 + (b + 1) * 128], qfp[:, b * 128:(b + 1) * 128], identb[:])
                pinc(ins, f"qtr_{tag}")
                if prev_own is None:
                    tensor.wait_ge(sems["S"], sch.val("S", "mir_boot"))
                else:
                    tensor.wait_ge(sems["S"], sch.val("S", f"mir_{prev_own}"))
                    tensor.wait_ge(sems["V"], sch.val("V", f"adq_{prev_own}_15"))
                for half2 in range(2):
                    for gq in (2 * half2, 2 * half2 + 1):
                        tensor.wait_ge(sems["V"], sch.val("V", f"q2b_{tag}_{gq // 2}"))
                        if gq >= 1:
                            tensor.wait_ge(sems["S"], sch.val("S", f"q2f_{tag}_{gq - 1}"))
                        else:
                            tensor.wait_ge(sems["S"], sch.val("S", f"qtm_{tag}"))
                        for t8 in range(8):
                            idx = (gq % 2) * 8 + t8
                            ins = tensor.transpose(p_tr[:, t8 * 128:(t8 + 1) * 128],
                                                   q2t[:, idx * 128:(idx + 1) * 128], identb[:])
                        pinc(ins, f"q2t_{tag}_{gq}")
                    for h in range(8 * half2, 8 * half2 + 8):
                        b, s = h // 4, h % 4
                        if h == 0:
                            tensor.wait_ge(sems["S"], sch.val("S", f"qfp_{tag}"))
                        if h in (8, 9):
                            tensor.wait_ge(sems["V"], sch.val("V", f"yd_{tag}_{6 + (h % 2)}"))
                        if h >= 2:
                            tensor.wait_ge(sems["S"], sch.val("S", f"psq_{tag}_{h - 2}"))
                        elif prev_own is not None:
                            tensor.wait_ge(sems["S"], sch.val("S", f"psq_{prev_own}_{h + 14}"))
                        ins = tensor.matmul(SCB(h),
                                            lhsT=kfp[32 * s:32 * s + 16, b * 128:(b + 1) * 128],
                                            rhs=qfp[32 * s:32 * s + 16, b * 128:(b + 1) * 128], start=True, stop=True,
                                            tile_position=(32 * s, 0))
                        pinc(ins, f"sc_{tag}_{h}")
                    for grp in (2 * half2, 2 * half2 + 1):
                        if prev_own is not None:
                            tensor.wait_ge(sems["S"], sch.val("S", f"yev_{prev_own}_{grp}"))
                        tensor.wait_ge(sems["S"], sch.val("S", f"q2f_{tag}_{grp}"))
                        for s4 in range(4):
                            h = grp * 4 + s4
                            b, s = h // 4, h % 4
                            tensor.wait_ge(sems["G"], sch.val("G", f"attp_{tag}_{h // 2}"))
                            yslot = YSL(h)
                            tensor.matmul(yslot, lhsT=vga[:, h * 65:(h + 1) * 65],
                                          rhs=att[:, (h % 8) * 128:(h % 8) * 128 + 128], start=True, stop=False)
                            tensor.matmul(yslot, lhsT=sb1b[32 * s:32 * s + 17, b * 65:(b + 1) * 65],
                                          rhs=qfp[32 * s:32 * s + 17, b * 128:(b + 1) * 128], start=False, stop=False,
                                          tile_position=(32 * s, 0))
                            for t2 in range(2):
                                ins = tensor.matmul(yslot, lhsT=sqb[:, h * 130 + t2 * 65:h * 130 + (t2 + 1) * 65],
                                                    rhs=q2f[:, (grp % 2) * 1024 + (s4 * 2 + t2) * 128:(grp % 2) * 1024 + (s4 * 2 + t2 + 1) * 128],
                                                    start=False, stop=(t2 == 1))
                        pinc(ins, f"ymm_{tag}_{grp}")
                    for h in range(8 * half2, 8 * half2 + 8):
                        grp = h // 4
                        if h % 8 == 0:
                            tensor.wait_ge(sems["S"], sch.val("S", f"psq_{tag}_{8 * half2 + 7}"))
                        tensor.wait_ge(sems["V"], sch.val("V", f"rd_{tag}_{grp}"))
                        if h >= 2:
                            tensor.wait_ge(sems["V"], sch.val("V", f"yd_{tag}_{h - 2}"))
                        elif prev_own is not None:
                            tensor.wait_ge(sems["V"], sch.val("V", f"yd_{prev_own}_{h + 14}"))
                        ins = tensor.matmul(RSL(h),
                                            lhsT=ones64[0:1, 0:64],
                                            rhs=rden[0:1, (grp % 2) * 512 + (h % 4) * 128:(grp % 2) * 512 + (h % 4) * 128 + 128],
                                            start=True, stop=True)
                        pinc(ins, f"rep_{tag}_{h}")
                sup()
                tensor.wait_ge(sems["V"], sch.val("V", f"yd_{tag}_15"))
                for half in range(2):
                    bank = p_pa if half == 0 else p_pb
                    for k in range(8):
                        ins = tensor.matmul(bank[:, 0:512], lhsT=yf[:, k * 128:(k + 1) * 128],
                                            rhs=wo_s[:, k * 1024 + half * 512:k * 1024 + half * 512 + 512],
                                            start=(k == 0), stop=(k == 7))
                    pinc(ins, f"ao_{tag}_{half}")
                tensor.wait_ge(sems["V"], sch.val("V", f"xq_{tag}"))
                tensor.wait_ge(sems["S"], sch.val("S", f"q2f_{tag}_3"))
                for t8 in range(8):
                    ins = tensor.transpose(p_tr[:, t8 * 128:(t8 + 1) * 128],
                                           xq2[:, (c % 2) * D + t8 * 128:(c % 2) * D + (t8 + 1) * 128], identb[:])
                pinc(ins, f"xqt_{tag}")
                g, c2 = c // 2, c % 2
                if c2 == 1:
                    for sp in range(6):
                        sw = SW(sp)
                        for c4 in range(2):
                            t = sp * 2 + c4
                            if c4 == 0:
                                tensor.wait_ge(sems["Wg00"], sch.val("Wg00", f"gu_{g}_{sp}_0_0"))
                                tensor.wait_ge(sems["Wg10"], sch.val("Wg10", f"gu_{g}_{sp}_0_1"))
                                if sp >= 1:
                                    tensor.wait_ge(sems["S"], sch.val("S", f"sil_{g}_{2 * sp - 1}"))
                                elif g >= 1:
                                    tensor.wait_ge(sems["S"], sch.val("S", f"sil_{g - 1}_11"))
                                    tensor.wait_ge(sems["V"], sch.val("V", f"od_{g - 1}_0_1"))
                                    tensor.wait_ge(sems["V"], sch.val("V", f"od_{g - 1}_1_1"))
                            else:
                                tensor.wait_ge(sems["S"], sch.val("S", f"sil_{g}_{2 * sp}"))
                            tensor.wait_ge(sems["S"], sch.val("S", f"xqf_o{2 * g + c4}"))
                            for k in range(8):
                                hk = k // 4
                                ins = tensor.matmul(p_fa[:, 0:sw],
                                                    lhsT=xq2[:, c4 * D + k * 128:c4 * D + (k + 1) * 128],
                                                    rhs=gu_buf[:, hk * 2048 + (k % 4) * sw:hk * 2048 + (k % 4) * sw + sw],
                                                    start=(k == 0), stop=(k == 7))
                            pinc(ins, f"gt_{g}_{t}")
                        for c4 in range(2):
                            t = sp * 2 + c4
                            if c4 == 0:
                                tensor.wait_ge(sems["Wg01"], sch.val("Wg01", f"gu_{g}_{sp}_1_0"))
                                tensor.wait_ge(sems["Wg11"], sch.val("Wg11", f"gu_{g}_{sp}_1_1"))
                                if sp >= 1:
                                    tensor.wait_ge(sems["V"], sch.val("V", f"hm_{g}_{2 * sp - 1}"))
                                elif g >= 1:
                                    tensor.wait_ge(sems["V"], sch.val("V", f"hq_{g - 1}_2"))
                            else:
                                tensor.wait_ge(sems["V"], sch.val("V", f"hm_{g}_{2 * sp}"))
                            for k in range(8):
                                hk = k // 4
                                ins = tensor.matmul(p_fb[:, 0:sw],
                                                    lhsT=xq2[:, c4 * D + k * 128:c4 * D + (k + 1) * 128],
                                                    rhs=gu_buf[:, hk * 2048 + (k % 4) * sw:hk * 2048 + (k % 4) * sw + sw],
                                                    start=(k == 0), stop=(k == 7))
                            pinc(ins, f"up_{g}_{sp}_{c4}")
                    for c4 in range(2):
                        for bq in range(3):
                            nt = 8 if bq < 2 else 6
                            tensor.wait_ge(sems["V"], sch.val("V", f"hq_{g}_{bq}"))
                            if bq >= 1:
                                tensor.wait_ge(sems["S"], sch.val("S", f"hqf_{g}_{c4}_{bq - 1}"))
                            elif c4 == 0:
                                tensor.wait_ge(sems["S"], sch.val("S", f"xqf_{tag}"))
                            else:
                                tensor.wait_ge(sems["S"], sch.val("S", f"hqf_{g}_0_2"))
                            for t8 in range(nt):
                                idx = bq * 8 + t8
                                ins = tensor.transpose(p_tr[:, t8 * 128:(t8 + 1) * 128],
                                                       hq_t[:, c4 * IF + idx * 128:c4 * IF + (idx + 1) * 128], identb[:])
                            pinc(ins, f"hqt_{g}_{c4}_{bq}")
                    for nh in range(2):
                        for kb in range(2):
                            tensor.wait_ge(sems["Wd"], sch.val("Wd", f"wd_{g}_{nh}_{kb}"))
                            for c4 in range(2):
                                bank = p_fa if c4 == 0 else p_fb
                                if nh == 0 and kb == 0:
                                    tensor.wait_ge(sems["V"], sch.val("V", f"hq_{g}_2"))
                                    tensor.wait_ge(sems["S"], sch.val("S", f"hqf_{g}_{c4}_2"))
                                if nh == 1 and kb == 0:
                                    tensor.wait_ge(sems["V"], sch.val("V", f"od_{g}_{c4}_0"))
                                for j in range(11):
                                    kk = kb * 11 + j
                                    ins = tensor.matmul(bank[:, 0:512],
                                                        lhsT=hq_t[:, c4 * IF + kk * 128:c4 * IF + (kk + 1) * 128],
                                                        rhs=wd_buf[:, j * 512:(j + 1) * 512],
                                                        start=(kb == 0 and j == 0), stop=(kb == 1 and j == 10))
                                if kb == 1:
                                    pinc(ins, f"dnm_{g}_{c4}_{nh}")
                            if kb == 0:
                                pinc(ins, f"dnb_{g}_{nh}_{kb}")
                            else:
                                dr = tensor.drain(fusable=True)
                                dr.then_inc(sems["P"], 1)
                                sch.inc("P", f"dnb_{g}_{nh}_{kb}")
                wq_transposes(tensor, cc)
                prev = tag
                prev_own = tag

        def wq_transposes(tensor, cc):
            prev_sp = pre_wq_tr_label(cc)
            for (mat, r0, nr) in WQ_AT.get(cc, []):
                W = MATD[mat][2]
                ncg = W // 128
                nbat = (ncg + 3) // 4
                for bt in range(nbat):
                    cg0 = bt * 4
                    ncg_b = min(4, ncg - cg0)
                    tensor.wait_ge(sems["V"], sch.val("V", f"wqq_{WQ_UNIT_LAST[(mat, r0)]}"))
                    if bt >= 1:
                        tensor.wait_ge(sems["S"], sch.val("S", f"wsp_{mat}{r0}_b{bt - 1}"))
                    else:
                        tensor.wait_ge(sems["S"], sch.val("S", prev_sp))
                    prev_sp = f"wsp_{mat}{r0}_b{nbat - 1}" if bt == nbat - 1 else prev_sp
                    for cgi in range(ncg_b):
                        cg = cg0 + cgi
                        for j in range(nr):
                            ins = tensor.transpose(p_tr[:, (cgi * nr + j) * 128:(cgi * nr + j + 1) * 128],
                                                   wqn[:, j * W + cg * 128:j * W + (cg + 1) * 128], identb[:])
                    pinc(ins, f"wqt_{mat}{r0}_b{bt}")

        fe = FakeEng()
        sch.record = True
        for fn in (sync_fn, vec_fn, gp_fn, sc_fn, pe_fn):
            sch.counts = dict(sch.counts)
            fn(fe)
        sch.record = False
        block = es.enter_context(nc.Block())

        @block.sync
        def _(sync):
            sync_fn(sync)

        @block.vector
        def _(vector):
            vec_fn(vector)

        @block.gpsimd
        def _(gpsimd):
            gpsimd.memset(nhalf[:], -0.5)
            gpsimd.drain()
            gp_fn(gpsimd)

        @block.scalar
        def _(scalar):
            sc_fn(scalar)

        @block.tensor
        def _(tensor):
            pe_fn(tensor)

    _LAST_SCHED['labels'] = dict(sch.labels)
    return nc



def _prep_weights(Wq, Wk):
    WqTp = np.zeros((D, 512), np.float32)
    WkTp = np.zeros((D, 512), np.float32)
    for h in range(16):
        b, s = h // 4, h % 4
        WqTp[:, b * 128 + 32 * s:b * 128 + 32 * s + 16] = Wq[h * 16:(h + 1) * 16, :].T
        WkTp[:, b * 128 + 32 * s:b * 128 + 32 * s + 16] = Wk[h * 16:(h + 1) * 16, :].T
    return WqTp, WkTp


_CACHE = {}


def kernel(L0_attn_hidden_states, L0_attn_Wq, L0_attn_Wk, L0_attn_Wv, L0_attn_Wo,
           L0_ffn_w_gate, L0_ffn_w_up, L0_ffn_w_down):
    x = np.ascontiguousarray(np.asarray(L0_attn_hidden_states, np.float32))
    Wq = np.asarray(L0_attn_Wq, np.float32)
    Wk = np.asarray(L0_attn_Wk, np.float32)
    Wv = np.asarray(L0_attn_Wv, np.float32)
    Wo = np.asarray(L0_attn_Wo, np.float32)
    wg = np.ascontiguousarray(np.asarray(L0_ffn_w_gate, np.float32))
    wu = np.ascontiguousarray(np.asarray(L0_ffn_w_up, np.float32))
    wd = np.ascontiguousarray(np.asarray(L0_ffn_w_down, np.float32))

    WqTp, WkTp = _prep_weights(Wq, Wk)
    WvT = np.ascontiguousarray(Wv.T)
    WoT = np.ascontiguousarray(Wo.T)
    ident = np.eye(128, dtype=np.float32)
    onehot = np.zeros((128, 1), np.float32)
    for s in range(4):
        onehot[32 * s + 16, 0] = 1.0

    if "nc" not in _CACHE:
        _CACHE["nc"] = build_program()
    nc = _CACHE["nc"]

    in_maps = []
    for core in range(8):
        b, half = core // 2, core % 2
        m = {
            "x_own": np.ascontiguousarray(x[b, half * 2048:(half + 1) * 2048]),
            "x_rec": np.ascontiguousarray(x[b, 0:2048]),
            "gatev": np.full((128, 1), float(half), np.float32),
            "onehot": onehot,
            "ident": ident,
            "WqTp": WqTp, "WkTp": WkTp, "WvT": WvT, "WoT": WoT,
            "w_gate": wg, "w_up": wu, "w_down": wd,
        }
        in_maps.append(m)
    res = run_bass_kernel_spmd(nc, in_maps, core_ids=list(range(8)))
    out = np.empty((4, 4096, D), np.float32)
    for core in range(8):
        b, half = core // 2, core % 2
        out[b, half * 2048:(half + 1) * 2048] = res.results[core]["out"]
    return out


# version 25
# speedup vs baseline: 1.0289x; 1.0289x over previous
import numpy as np
import ml_dtypes
from contextlib import ExitStack
import concourse.bass as bass
import concourse.mybir as mybir
from concourse.bass_utils import run_bass_kernel_spmd

F32 = mybir.dt.float32
BF16 = mybir.dt.bfloat16
ALU = mybir.AluOpType
AF = mybir.ActivationFunctionType
AX = mybir.AxisListType

D = 1024
CH = 128
IF = 2816
LN_EPS = 1e-5
MSQ = [0.0625, 0.5625, 1.5625, 3.0625, 6.25, 12.25, 25.0]
N_REC = 16
N_OWN = 16


class Sched:
    def __init__(self):
        self.record = True
        self.labels = {}
        self.counts = {}

    def inc(self, sem, label, amount=1):
        if self.record:
            v = self.counts.get(sem, 0) + amount
            self.counts[sem] = v
            assert (sem, label) not in self.labels, ("dup", sem, label)
            self.labels[(sem, label)] = v

    def val(self, sem, label):
        if self.record:
            return 1
        assert (sem, label) in self.labels, ("missing", sem, label)
        return self.labels[(sem, label)]


class FakeIns:
    def then_inc(self, *a, **k):
        return self


class FakeEng:
    def __getattr__(self, name):
        if name in ("wait_ge",):
            return lambda *a, **k: None
        return lambda *a, **k: FakeIns()


_LAST_SCHED = {}


def build_program(n_rec=N_REC, n_own=N_OWN):
    nc = bass.Bass("TRN2", target_bir_lowering=False)
    T_OWN, T_REC = n_own * CH, n_rec * CH

    dx = nc.dram_tensor("x_own", [T_OWN, D], F32, kind="ExternalInput").ap()
    dxr = nc.dram_tensor("x_rec", [T_REC, D], F32, kind="ExternalInput").ap()
    dgate = nc.dram_tensor("gatev", [128, 1], F32, kind="ExternalInput").ap()
    donehot = nc.dram_tensor("onehot", [128, 1], F32, kind="ExternalInput").ap()
    dident = nc.dram_tensor("ident", [128, 128], F32, kind="ExternalInput").ap()
    dwq = nc.dram_tensor("WqTp", [D, 512], F32, kind="ExternalInput").ap()
    dwk = nc.dram_tensor("WkTp", [D, 512], F32, kind="ExternalInput").ap()
    dwv = nc.dram_tensor("WvT", [D, D], F32, kind="ExternalInput").ap()
    dwo = nc.dram_tensor("WoT", [D, D], F32, kind="ExternalInput").ap()
    dwg = nc.dram_tensor("w_gate", [IF, D], F32, kind="ExternalInput").ap()
    dwu = nc.dram_tensor("w_up", [IF, D], F32, kind="ExternalInput").ap()
    dwd = nc.dram_tensor("w_down", [D, IF], F32, kind="ExternalInput").ap()
    dout = nc.dram_tensor("out", [T_OWN, D], F32, kind="ExternalOutput").ap()
    dgq = nc.dram_tensor("wgTq", [D, IF], BF16, kind="Internal").ap()
    duq = nc.dram_tensor("wuTq", [D, IF], BF16, kind="Internal").ap()
    ddq = nc.dram_tensor("wdTq", [IF, D], BF16, kind="Internal").ap()
    MATD = {"g": (dwg, dgq, D), "u": (dwu, duq, D), "d": (dwd, ddq, IF)}

    es = ExitStack()
    sch = Sched()
    with es:
        S_ = lambda n, sh, dt: es.enter_context(nc.sbuf_tensor(n, sh, dt))
        P_ = lambda n, sh, dt=F32: es.enter_context(nc.psum_tensor(n, sh, dt))

        identb = S_("identb", [128, 128], BF16)
        gate = S_("gate_s", [128, 1], F32)
        onehot = S_("onehot_s", [128, 1], F32)
        ones64 = S_("ones64", [1, 64], BF16)
        six = S_("six", [128, 1], F32)
        four = S_("four", [128, 1], F32)
        r6 = S_("r6", [128, 1], F32)
        wq_s = S_("wq_s", [128, 4096], BF16)
        wk_s = S_("wk_s", [128, 4096], BF16)
        wv_s = S_("wv_s", [128, 8192], BF16)
        wo_s = S_("wo_s", [128, 8192], BF16)
        wstage = S_("wstage", [128, 2048], F32)
        wqn = S_("wqn", [128, IF], BF16)
        wsp = S_("wsp", [128, 512], BF16)
        gu_buf = S_("gu_buf", [128, 4096], BF16)
        wd_buf = S_("wd_buf", [128, 5632], BF16)
        xt = S_("xt", [128, 2 * D], F32)
        bn1 = S_("bn1", [128, 12], F32)
        mv1 = S_("mv1", [128, 2], F32)
        veps1 = S_("veps1", [128, 1], F32)
        rstd1 = S_("rstd1", [128, 1], F32)
        nbias1 = S_("nbias1", [128, 1], F32)
        xn1 = S_("xn1", [128, D], BF16)
        xnf = S_("xnf", [128, D], BF16)
        qfp = S_("qfp", [128, 512], BF16)
        kfp = S_("kfp", [128, 512], BF16)
        ktm = S_("ktm", [128, 512], BF16)
        k32 = S_("k32", [128, 512], BF16)
        qtm = S_("qtm", [128, 512], BF16)
        kf = S_("kf", [128, 16 * 273], BF16)
        q2t = S_("q2t", [128, 2048], BF16)
        q2f = S_("q2f", [128, 2048], BF16)
        vga = S_("vga", [128, 16 * 65], BF16)
        att = S_("att", [128, 1024], BF16)
        psc = S_("psc", [128, 256], F32)
        sb1 = S_("sb1", [128, 260], F32)
        sb1b = S_("sb1b", [128, 260], BF16)
        sq_ = S_("sq_", [128, 2080], F32)
        sqb = S_("sqb", [128, 2080], BF16)
        ysb = S_("ysb", [128, 1024], F32)
        rden = S_("rden", [1, 1024], BF16)
        yf = S_("yf", [128, 1024], BF16)
        x1 = S_("x1", [128, 2 * D], F32)
        bn2 = S_("bn2", [128, 12], F32)
        mv2 = S_("mv2", [128, 2], F32)
        veps2 = S_("veps2", [128, 1], F32)
        rstd2 = S_("rstd2", [128, 1], F32)
        nbias2 = S_("nbias2", [128, 1], F32)
        xn2 = S_("xn2", [128, D], F32)
        xq2 = S_("xq2", [128, 2 * D], BF16)
        gsil = S_("gsil", [128, 512], F32)
        hq_t = S_("hq_t", [128, 2 * IF], BF16)
        outb = S_("outb", [128, 2 * D], F32)
        q_m = S_("q_m", [128, 128], F32)
        q_sh = S_("q_sh", [128, 128], F32)
        q_r = S_("q_r", [128, 128], F32)
        q_t = S_("q_t", [128, 2048], F32)
        q_u = S_("q_u", [128, 2048], F32)
        q_a = S_("q_a", [128, 2048], BF16)
        q_b = S_("q_b", [128, 2048], BF16)

        p_sc = P_("p_sc", [128, 512])
        p_y = P_("p_y", [128, 512])
        p_pa = P_("p_pa", [128, 512])
        p_ds = P_("p_ds", [128, 512])
        p_pb = P_("p_pb", [128, 512])
        p_fa = P_("p_fa", [128, 512])
        p_fb = P_("p_fb", [128, 512])
        p_tr = P_("p_tr", [128, 1024], BF16)

        sems = {n: es.enter_context(nc.semaphore(n)) for n in
                ["P", "V", "S", "G", "Dk", "Dw", "Dw0", "Dw1", "Dx0", "Dx1", "Do0", "Do1",
                 "Ws", "Wg00", "Wg01", "Wg10", "Wg11", "Wd"]}

        ALLC = [("r", c) for c in range(n_rec)] + [("o", c) for c in range(n_own)]
        TAG = {cc: f"{cc[0]}{cc[1]}" for cc in ALLC}

        WQ_UNITS = []
        for mat, nrt in (("d", 8), ("g", 22), ("u", 22)):
            r = 0
            while r < nrt:
                WQ_UNITS.append((mat, r, 1))
                r += 1
        early = ALLC[:n_rec + 1] if len(ALLC) > n_rec else ALLC
        WQ_AT = {}
        for i, u in enumerate(WQ_UNITS):
            WQ_AT.setdefault(early[i % len(early)], []).append(u)
        WQ_ORDER = []
        for cc in ALLC:
            WQ_ORDER.extend(WQ_AT.get(cc, []))
        WQ_PREV = {}
        for i, u in enumerate(WQ_ORDER):
            WQ_PREV[(u[0], u[1])] = WQ_ORDER[i - 1] if i >= 1 else None
        WQ_SLICES = []
        for (mat, r0, nr) in WQ_ORDER:
            W = MATD[mat][2]
            nsl = (W + 1023) // 1024
            for j in range(nr):
                for sl in range(nsl):
                    WQ_SLICES.append((mat, r0, j, sl))
        WQ_SLICE_IDX = {k: i for i, k in enumerate(WQ_SLICES)}
        WQ_LADDERS = {}
        WQ_LOADS = {}
        WQ_UNIT_LAST = {}
        WQ_DSTOFF = {}
        _slot_reader = {0: None, 1: None}
        _next_slot = 0
        for cc in ALLC:
            lads = WQ_LADDERS.setdefault(cc, [])
            loads = WQ_LOADS.setdefault(cc, [])
            us = WQ_AT.get(cc, [])
            ui = 0
            while ui < len(us):
                (mat, r0, nr) = us[ui]
                if mat == "d":
                    WQ_DSTOFF[(mat, r0)] = 0
                    lads.append(([(mat, r0, nr)], f"{mat}{r0}_0p",
                                 [(mat, r0, 0, 0, 0, 1024, 0), (mat, r0, 0, 1, 1024, 2048, 1)],
                                 0, 2048))
                    lads.append(([(mat, r0, nr)], f"{mat}{r0}_2",
                                 [(mat, r0, 0, 2, 2048, 2816, 0)], 2048, 768))
                    _next_slot = 1
                    ui += 1
                elif ui + 1 < len(us) and us[ui + 1][0] != "d":
                    (m2, r2, n2) = us[ui + 1]
                    WQ_DSTOFF[(mat, r0)] = 0
                    WQ_DSTOFF[(m2, r2)] = 1024
                    lads.append(([(mat, r0, nr), (m2, r2, n2)], f"{mat}{r0}x{m2}{r2}",
                                 [(mat, r0, 0, 0, 0, 1024, 0), (m2, r2, 0, 0, 0, 1024, 1)],
                                 0, 2048))
                    _next_slot = 0
                    ui += 2
                else:
                    WQ_DSTOFF[(mat, r0)] = 0
                    lads.append(([(mat, r0, nr)], f"{mat}{r0}_0",
                                 [(mat, r0, 0, 0, 0, 1024, _next_slot)], 0, 1024))
                    _next_slot ^= 1
                    ui += 1
            for (units, lbl, segs, dst0, Wtot) in lads:
                for (sm, sr, jj, ssl, w0, w1, slot) in segs:
                    loads.append((sm, sr, jj, ssl, w0, w1, slot,
                                  _slot_reader[slot], f"wql_{sm}{sr}_{jj}_{ssl}"))
                for (sm, sr, jj, ssl, w0, w1, slot) in segs:
                    _slot_reader[slot] = f"qt_{lbl}"
                for (um, ur, un) in units:
                    WQ_UNIT_LAST[(um, ur)] = lbl
        last_spill = {}
        for (mat, r0, nr) in WQ_UNITS:
            W = MATD[mat][2]
            nbat = (W // 128 + 3) // 4
            last_spill[mat] = f"{mat}{r0}_b{nbat - 1}"
        lmat, lr0, _ = WQ_ORDER[-1]
        lW = MATD[lmat][2]
        lnbat = (lW // 128 + 3) // 4
        lncgb = min(4, lW // 128 - (lnbat - 1) * 4)
        LAST_SPILL_ALL = f"wqd_{lmat}{lr0}_b{lnbat - 1}_{lncgb - 1}"

        WLOADS = []
        for nm, src, dst, cols in (("wq", dwq, wq_s, 512), ("wk", dwk, wk_s, 512),
                                   ("wv", dwv, wv_s, 1024), ("wo", dwo, wo_s, 1024)):
            npieces = 8 * cols // 1024
            for p in range(npieces):
                WLOADS.append((f"{nm}{p}", src, dst, cols, p))

        def SW(sp):
            return 512 if sp < 5 else 256

        def q_vec(vector, src, dst, W, lbl):
            ng = W // 16
            g3 = src.rearrange("p (g w) -> p g w", w=16)
            vector.tensor_reduce(q_m[:, 0:ng], g3, axis=AX.X, op=ALU.max, apply_absolute_value=True)
            vector.drain()
            vector.reciprocal(q_r[:, 0:ng], q_m[:, 0:ng])
            vector.tensor_scalar(out=q_sh[:, 0:ng], in0=q_m[:, 0:ng], scalar1=1.0 / 12.0, scalar2=None,
                                 op0=ALU.mult)
            vector.drain()
            rb = q_r[:, 0:ng].to_broadcast([128, ng, 16])
            shb = q_sh[:, 0:ng].to_broadcast([128, ng, 16])
            ins = vector.tensor_tensor(out=q_t[:, 0:W].rearrange("p (g w) -> p g w", w=16),
                                       in0=g3, in1=rb, op=ALU.mult)
            ins.then_inc(sems["V"], 1)
            sch.inc("V", f"qt_{lbl}")
            vector.wait_ge(sems["S"], sch.val("S", f"qu_{lbl}"))
            vector.tensor_scalar(out=q_a[:, 0:W], in0=q_u[:, 0:W], scalar1=MSQ[0], scalar2=None, op0=ALU.is_gt)
            vector.tensor_scalar(out=q_b[:, 0:W], in0=q_u[:, 0:W], scalar1=MSQ[2], scalar2=None, op0=ALU.is_gt)
            vector.tensor_scalar(out=q_t[:, 0:W], in0=q_u[:, 0:W], scalar1=MSQ[4], scalar2=None, op0=ALU.is_gt)
            vector.drain()
            vector.scalar_tensor_tensor(out=q_a[:, 0:W], in0=q_u[:, 0:W], scalar=MSQ[1], in1=q_a[:, 0:W],
                                        op0=ALU.is_gt, op1=ALU.add)
            vector.scalar_tensor_tensor(out=q_b[:, 0:W], in0=q_u[:, 0:W], scalar=MSQ[3], in1=q_b[:, 0:W],
                                        op0=ALU.is_gt, op1=ALU.add)
            vector.scalar_tensor_tensor(out=q_t[:, 0:W], in0=q_u[:, 0:W], scalar=MSQ[5], in1=q_t[:, 0:W],
                                        op0=ALU.is_gt, op1=ALU.add)
            vector.tensor_scalar(out=q_u[:, 0:W], in0=q_u[:, 0:W], scalar1=MSQ[6], scalar2=4.0,
                                 op0=ALU.is_gt, op1=ALU.mult)
            vector.drain()
            vector.tensor_tensor(out=q_a[:, 0:W], in0=q_a[:, 0:W], in1=q_b[:, 0:W], op=ALU.add)
            vector.scalar_tensor_tensor(out=q_t[:, 0:W], in0=q_t[:, 0:W], scalar=2.0, in1=q_u[:, 0:W],
                                        op0=ALU.mult, op1=ALU.add)
            vector.drain()
            vector.tensor_tensor(out=q_a[:, 0:W], in0=q_a[:, 0:W], in1=q_t[:, 0:W], op=ALU.add)
            vector.tensor_tensor(out=q_b[:, 0:W].rearrange("p (g w) -> p g w", w=16),
                                 in0=dst, in1=shb, op=ALU.mult)
            vector.drain()
            return vector.tensor_tensor(out=dst, in0=q_a[:, 0:W], in1=q_b[:, 0:W], op=ALU.mult)

        def q_sc(scalar, lbl, W, dst):
            scalar.wait_ge(sems["V"], sch.val("V", f"qt_{lbl}"))
            ins = scalar.activation(dst, q_t[:, 0:W], AF.Sign)
            ins.then_inc(sems["S"], 1)
            sch.inc("S", f"qg_{lbl}")
            ins = scalar.activation(q_u[:, 0:W], q_t[:, 0:W], AF.Square, scale=6.0)
            ins.then_inc(sems["S"], 1)
            sch.inc("S", f"qu_{lbl}")

        def wq_unit_slices(cc):
            for (mat, r0, nr) in WQ_AT.get(cc, []):
                W = MATD[mat][2]
                nsl = (W + 1023) // 1024
                for j in range(nr):
                    for sl in range(nsl):
                        w0 = sl * 1024
                        w1 = min(w0 + 1024, W)
                        yield mat, r0, nr, j, sl, w0, w1, W

        def wq_units_sc(scalar, cc, ladders=None, qsc_only=False, skip_first_qsc=False):
            lads = WQ_LADDERS.get(cc, []) if ladders is None else ladders
            for li, (units, lbl, segs, dst0, Wtot) in enumerate(lads):
                if not (skip_first_qsc and li == 0):
                    q_sc(scalar, lbl, Wtot, wqn[:, dst0:dst0 + Wtot])
                if qsc_only:
                    continue
                for (um, ur, un) in units:
                    if WQ_UNIT_LAST[(um, ur)] == lbl:
                        wq_spill_evacs_unit(scalar, cc, um, ur, un)

        def sync_fn(sync):
            sync.dma_start(out=wstage[:, 0:128], in_=dident[:]).then_inc(sems["Dk"], 16)
            sync.dma_start(out=gate[:], in_=dgate[:]).then_inc(sems["Dk"], 16)
            sync.dma_start(out=onehot[:], in_=donehot[:]).then_inc(sems["Dk"], 16)
            sch.inc("Dk", "konst", 48)
            for i, (lbl, src, dst, cols, p) in enumerate(WLOADS):
                if i >= 2:
                    sync.wait_ge(sems["V"], sch.val("V", "wc_" + WLOADS[i - 2][0]))
                elif i == 0:
                    sync.wait_ge(sems["V"], sch.val("V", "idb"))
                kpp = 1024 // cols
                ap = src[p * kpp * 128:(p + 1) * kpp * 128, :].rearrange("(k p2) c -> p2 k c", p2=128)
                dst2 = wstage[:, (i % 2) * 1024:(i % 2) * 1024 + 1024].rearrange("p (k c) -> p k c", c=cols)
                sync.dma_start(out=dst2, in_=ap).then_inc(sems["Dw%d" % (i % 2)], 16)
                sch.inc("Dw%d" % (i % 2), "wl_" + lbl, 16)
            wq_prev = "wc_" + WLOADS[-1][0]
            for ci, cc in enumerate(ALLC):
                ph, c = cc
                tag = TAG[cc]
                if ci >= 2:
                    p2h, p2c = ALLC[ci - 2]
                    if p2h == "o":
                        sync.wait_ge(sems["V"], sch.val("V", f"x1b_o{p2c}"))
                    else:
                        sync.wait_ge(sems["S"], sch.val("S", f"ln1_r{p2c}"))
                src = dxr if ph == "r" else dx
                dxsem = "Dx%d" % (ci % 2)
                sync.dma_start(out=xt[:, (ci % 2) * D:(ci % 2 + 1) * D],
                               in_=src[c * CH:(c + 1) * CH, :]).then_inc(sems[dxsem], 16)
                sch.inc(dxsem, f"x_{tag}", 16)
                for (mat, r0, j, sl, w0, w1, slot, gate_lbl, load_lbl) in WQ_LOADS.get(cc, []):
                    dsrc = MATD[mat][0]
                    sync.wait_ge(sems["V"], sch.val("V", gate_lbl if gate_lbl is not None else wq_prev))
                    sync.dma_start(out=wstage[:, slot * 1024:slot * 1024 + w1 - w0],
                                   in_=dsrc[(r0 + j) * 128:(r0 + j + 1) * 128, w0:w1]).then_inc(sems["Dw%d" % slot], 16)
                    sch.inc("Dw%d" % slot, load_lbl, 16)
                if ph != "o":
                    continue
                g, c2 = c // 2, c % 2
                if c2 == 1:
                    for sp in range(6):
                        sw = SW(sp)
                        for mi, dq in ((0, dgq), (1, duq)):
                            for hk in range(2):
                                wgsem = "Wg%d%d" % (hk, mi)
                                if g == 0:
                                    sync.wait_ge(sems["Ws"], sch.val("Ws", LAST_SPILL_ALL))
                                if mi == 0:
                                    if not (g == 0 and sp == 0):
                                        lab = f"up_{g}_{sp - 1}_1" if sp >= 1 else f"up_{g - 1}_5_1"
                                        sync.wait_ge(sems["P"], sch.val("P", lab))
                                else:
                                    sync.wait_ge(sems["P"], sch.val("P", f"gt_{g}_{2 * sp + 1}"))
                                ap2 = dq[hk * 512:(hk + 1) * 512, sp * 512:sp * 512 + sw].rearrange("(k p2) c -> p2 k c", p2=128)
                                dgu = gu_buf[:, hk * 2048:hk * 2048 + 4 * sw].rearrange("p (k c) -> p k c", c=sw)
                                sync.dma_start(out=dgu, in_=ap2).then_inc(sems[wgsem], 16)
                                sch.inc(wgsem, f"gu_{g}_{sp}_{mi}_{hk}", 16)
                    for nh in range(2):
                        for kb in range(2):
                            if g == 0:
                                sync.wait_ge(sems["Ws"], sch.val("Ws", LAST_SPILL_ALL))
                            if not (g == 0 and nh == 0 and kb == 0):
                                pnh, pkb, pg = (nh, kb - 1, g) if kb >= 1 else ((nh - 1, 1, g) if nh >= 1 else (1, 1, g - 1))
                                sync.wait_ge(sems["P"], sch.val("P", f"dnb_{pg}_{pnh}_{pkb}"))
                            kk0 = kb * 11
                            ap3 = ddq[kk0 * 128:(kk0 + 11) * 128, nh * 512:(nh + 1) * 512].rearrange("(k p2) c -> p2 k c", p2=128)
                            dwdb = wd_buf[:].rearrange("p (k c) -> p k c", c=512)
                            sync.dma_start(out=dwdb, in_=ap3).then_inc(sems["Wd"], 16)
                            sch.inc("Wd", f"wd_{g}_{nh}_{kb}", 16)
                    for c4 in range(2):
                        cgl = 2 * g + c4
                        sync.wait_ge(sems["V"], sch.val("V", f"od_{g}_{c4}_1"))
                        dosem = "Do%d" % c4
                        sync.dma_start(out=dout[cgl * CH:(cgl + 1) * CH, :],
                                       in_=outb[:, c4 * D:(c4 + 1) * D]).then_inc(sems[dosem], 16)
                        sch.inc(dosem, f"st_{cgl}", 16)
            sync.wait_ge(sems["Do0"], sch.val("Do0", f"st_{n_own - 2}"))
            sync.wait_ge(sems["Do1"], sch.val("Do1", f"st_{n_own - 1}"))

        def wq_units_vec(vector, cc, ladders=None):
            for (units, lbl, segs, dst0, Wtot) in (WQ_LADDERS.get(cc, []) if ladders is None else ladders):
                for (sm, sr, j, sl, w0, w1, slot) in segs:
                    dwsem = "Dw%d" % slot
                    vector.wait_ge(sems[dwsem], sch.val(dwsem, f"wql_{sm}{sr}_{j}_{sl}"))
                fm, fr = segs[0][0], segs[0][1]
                if segs[0][3] == 0 and WQ_PREV[(fm, fr)] is not None:
                    pmat, pr0, pnr = WQ_PREV[(fm, fr)]
                    pnbat = (MATD[pmat][2] // 128 + 3) // 4
                    vector.wait_ge(sems["P"], sch.val("P", f"wqt_{pmat}{pr0}_b{pnbat - 1}"))
                base = segs[0][6] * 1024
                ins = q_vec(vector, wstage[:, base:base + Wtot],
                            wqn[:, dst0:dst0 + Wtot], Wtot, lbl)
                ins.then_inc(sems["V"], 1)
                sch.inc("V", f"wqq_{lbl}")

        def vec_fn(vector):
            vector.memset(four[:], 4.0)
            vector.memset(ones64[:], 1.0)
            vector.memset(vga[:, 64::65], 1.0)
            vector.memset(kf[:, 16::273], 1.0)
            vector.memset(p_ds[:], 0.0)
            vector.memset(p_sc[:], 0.0)
            vector.memset(sb1[:], 0.0)
            vector.memset(sq_[:], 0.0)
            vector.drain()
            vector.wait_ge(sems["Dk"], sch.val("Dk", "konst"))
            ins = vector.tensor_copy(out=identb[:], in_=wstage[:, 0:128])
            ins.then_inc(sems["V"], 1)
            sch.inc("V", "idb")
            for i, (lbl, src, dst, cols, p) in enumerate(WLOADS):
                dwsem = "Dw%d" % (i % 2)
                vector.wait_ge(sems[dwsem], sch.val(dwsem, "wl_" + lbl))
                ins = vector.tensor_copy(out=dst[:, p * 1024:(p + 1) * 1024],
                                         in_=wstage[:, (i % 2) * 1024:(i % 2) * 1024 + 1024])
                ins.then_inc(sems["V"], 1)
                sch.inc("V", "wc_" + lbl)
            prev_own = None
            for ci, cc in enumerate(ALLC):
                ph, c = cc
                tag = TAG[cc]
                xs = xt[:, (ci % 2) * D:(ci % 2 + 1) * D]
                dxsem = "Dx%d" % (ci % 2)
                vector.wait_ge(sems[dxsem], sch.val(dxsem, f"x_{tag}"))
                vector.bn_stats(bn1[:, 0:6], xs[:, 0:512])
                vector.bn_stats(bn1[:, 6:12], xs[:, 512:1024])
                vector.drain()
                vector.bn_aggr(mv1[:], bn1[:])
                vector.drain()
                if ci >= 1:
                    vector.wait_ge(sems["G"], sch.val("G", f"rs1_{TAG[ALLC[ci - 1]]}"))
                    vector.wait_ge(sems["S"], sch.val("S", f"ln1_{TAG[ALLC[ci - 1]]}"))
                ins = vector.tensor_scalar(out=veps1[:], in0=mv1[:, 1:2], scalar1=LN_EPS, scalar2=None, op0=ALU.add)
                ins.then_inc(sems["V"], 1)
                sch.inc("V", f"lv1_{tag}")
                vector.wait_ge(sems["G"], sch.val("G", f"rs1_{tag}"))
                ins = vector.scalar_tensor_tensor(out=nbias1[:], in0=mv1[:, 0:1], scalar=-1.0, in1=rstd1[:],
                                                  op0=ALU.mult, op1=ALU.mult)
                ins.then_inc(sems["V"], 1)
                sch.inc("V", f"nb1_{tag}")
                if ph == "o":
                    ktm_vv = ktm[:].rearrange("p (bs j) -> p bs j", j=32)[:, :, 0:16]
                    kf_hv = kf[:].rearrange("p (h d) -> p h d", h=16)
                    vector.wait_ge(sems["S"], sch.val("S", f"ktm_{tag}"))
                    vector.wait_ge(sems["P"], sch.val("P", f"supq_{TAG[ALLC[ci - 1]]}_15"))
                    vector.tensor_scalar(out=k32[:], in0=ktm[:], scalar1=1.0 / 32, scalar2=None, op0=ALU.mult)
                    vector.drain()
                    vector.tensor_scalar(out=kf_hv[:, :, 0:16], in0=ktm_vv, scalar1=0.25, scalar2=None, op0=ALU.mult)
                    k32_vv = k32[:].rearrange("p (bs j) -> p bs j", j=32)
                    for i in range(16):
                        c1 = k32_vv[:, :, i:i + 1].to_broadcast([128, 16, 16])
                        ins = vector.tensor_tensor(out=kf_hv[:, :, 17 + 16 * i:33 + 16 * i],
                                                   in0=ktm_vv, in1=c1, op=ALU.mult)
                    ins.then_inc(sems["V"], 1)
                    sch.inc("V", f"kf_{tag}")
                    qtm_vv = qtm[:].rearrange("p (bs j) -> p bs j", j=32)[:, :, 0:16]
                    qtm_bv = qtm[:].rearrange("p (bs j) -> p bs j", j=32)
                    q2_hv = q2t[:].rearrange("p (h d) -> p h d", h=8)
                    for hh in range(2):
                        vector.wait_ge(sems["S"], sch.val("S", f"qtm_{tag}"))
                        if hh == 0:
                            if prev_own is not None:
                                vector.wait_ge(sems["P"], sch.val("P", f"q2t_{prev_own}_3"))
                        else:
                            vector.wait_ge(sems["P"], sch.val("P", f"q2t_{tag}_1"))
                        for i in range(16):
                            c1 = qtm_bv[:, hh * 8:(hh + 1) * 8, i:i + 1].to_broadcast([128, 8, 16])
                            ins = vector.tensor_tensor(out=q2_hv[:, :, 16 * i:16 * i + 16],
                                                       in0=qtm_vv[:, hh * 8:(hh + 1) * 8, :], in1=c1, op=ALU.mult)
                        ins.then_inc(sems["V"], 1)
                        sch.inc("V", f"q2b_{tag}_{hh}")

                def state_adds():
                    for grp in range(4):
                        vector.wait_ge(sems["P"], sch.val("P", f"supb_{tag}_{grp}"))
                        ins = vector.tensor_tensor(out=sb1[:, grp * 65:(grp + 1) * 65],
                                                   in0=sb1[:, grp * 65:(grp + 1) * 65],
                                                   in1=DBL(grp), op=ALU.add)
                        ins.then_inc(sems["V"], 1)
                        sch.inc("V", f"adb_{tag}_{grp}")
                    for h in range(16):
                        vector.wait_ge(sems["P"], sch.val("P", f"supq_{tag}_{h}"))
                        ins = vector.tensor_tensor(out=sq_[:, h * 130:(h + 1) * 130],
                                                   in0=sq_[:, h * 130:(h + 1) * 130],
                                                   in1=DQL(h), op=ALU.add)
                        ins.then_inc(sems["V"], 1)
                        sch.inc("V", f"adq_{tag}_{h}")

                if ph == "r":
                    lads = WQ_LADDERS.get(cc, [])
                    wq_units_vec(vector, cc, ladders=lads[:1])
                    state_adds()
                    wq_units_vec(vector, cc, ladders=lads[1:])
                    if ci == n_rec - 1:
                        vector.drain()
                        vector.tensor_scalar(out=sb1[:], in0=sb1[:], scalar1=gate[:], scalar2=None, op0=ALU.mult)
                        ins = vector.tensor_scalar(out=sq_[:], in0=sq_[:], scalar1=gate[:], scalar2=None, op0=ALU.mult)
                        ins.then_inc(sems["V"], 1)
                        sch.inc("V", "gated")
                    continue
                for grp in range(4):
                    vector.wait_ge(sems["S"], sch.val("S", f"yev_{tag}_{grp}"))
                    if grp >= 2:
                        vector.wait_ge(sems["P"], sch.val("P", f"rep_{tag}_{(grp - 2) * 4 + 3}"))
                    elif prev_own is not None:
                        vector.wait_ge(sems["P"], sch.val("P", f"rep_{prev_own}_{(grp + 2) * 4 + 3}"))
                    with nc.allow_low_precision(reason="bf16 recip-den feeds a bf16 matmul broadcast; tolerance budgeted"):
                        ins = vector.reciprocal(rden[0:1, (grp % 2) * 512:(grp % 2) * 512 + 512],
                                                ysb[64:65, (grp % 2) * 512:(grp % 2) * 512 + 512])
                    ins.then_inc(sems["V"], 1)
                    sch.inc("V", f"rd_{tag}_{grp}")
                    for hh in range(4):
                        h = grp * 4 + hh
                        if h == 0 and prev_own is not None:
                            vector.wait_ge(sems["P"], sch.val("P", f"ao_{prev_own}_1"))
                        vector.wait_ge(sems["P"], sch.val("P", f"rep_{tag}_{h}"))
                        ins = vector.tensor_tensor(
                            out=yf[64 * (h % 2):64 * (h % 2) + 64, (h // 2) * 128:(h // 2 + 1) * 128],
                            in0=ysb[0:64, (grp % 2) * 512 + hh * 128:(grp % 2) * 512 + hh * 128 + 128],
                            in1=RSL(h), op=ALU.mult)
                        ins.then_inc(sems["V"], 1)
                        sch.inc("V", f"yd_{tag}_{h}")
                state_adds()
                for half in range(2):
                    vector.wait_ge(sems["P"], sch.val("P", f"ao_{tag}_{half}"))
                    if half == 0 and c >= 2:
                        vector.wait_ge(sems["S"], sch.val("S", f"ln2_o{c - 2}"))
                    ins = vector.tensor_tensor(out=x1[:, (c % 2) * D + half * 512:(c % 2) * D + half * 512 + 512],
                                               in0=xs[:, half * 512:half * 512 + 512],
                                               in1=(p_pa if half == 0 else p_pb)[:, 0:512], op=ALU.add)
                    ins.then_inc(sems["V"], 1)
                    sch.inc("V", f"x1_{tag}_0" if half == 0 else f"x1b_{tag}")
                xv = x1[:, (c % 2) * D:(c % 2 + 1) * D]
                vector.drain()
                vector.bn_stats(bn2[:, 0:6], xv[:, 0:512])
                vector.bn_stats(bn2[:, 6:12], xv[:, 512:1024])
                vector.drain()
                vector.bn_aggr(mv2[:], bn2[:])
                vector.drain()
                if prev_own is not None:
                    vector.wait_ge(sems["G"], sch.val("G", f"rs2_{prev_own}"))
                    vector.wait_ge(sems["S"], sch.val("S", f"ln2_{prev_own}"))
                ins = vector.tensor_scalar(out=veps2[:], in0=mv2[:, 1:2], scalar1=LN_EPS, scalar2=None, op0=ALU.add)
                ins.then_inc(sems["V"], 1)
                sch.inc("V", f"lv2_{tag}")
                vector.wait_ge(sems["G"], sch.val("G", f"rs2_{tag}"))
                ins = vector.scalar_tensor_tensor(out=nbias2[:], in0=mv2[:, 0:1], scalar=-1.0, in1=rstd2[:],
                                                  op0=ALU.mult, op1=ALU.mult)
                ins.then_inc(sems["V"], 1)
                sch.inc("V", f"nb2_{tag}")
                vector.wait_ge(sems["S"], sch.val("S", f"ln2_{tag}"))
                if c >= 2:
                    vector.wait_ge(sems["P"], sch.val("P", f"up_{(c - 2) // 2}_5_{c % 2}"))
                ins = q_vec(vector, xn2[:], xq2[:, (c % 2) * D:(c % 2 + 1) * D], D, f"xq_{tag}")
                ins.then_inc(sems["V"], 1)
                sch.inc("V", f"xq_{tag}")
                g, c2 = c // 2, c % 2
                if c2 == 1:
                    for pq in range(3):
                        sp0, sp1 = 2 * pq, 2 * pq + 1
                        w0, w1 = SW(sp0), SW(sp1)
                        for sp in (sp0, sp1):
                            sw = SW(sp)
                            for c4 in range(2):
                                t = sp * 2 + c4
                                vector.wait_ge(sems["S"], sch.val("S", f"sil_{g}_{t}"))
                                vector.wait_ge(sems["P"], sch.val("P", f"up_{g}_{sp}_{c4}"))
                                off = c4 * (w0 + w1) + (w0 if sp == sp1 else 0)
                                ins = vector.tensor_tensor(out=q_t[:, off:off + sw],
                                                           in0=gsil[:, 0:sw], in1=p_fb[:, 0:sw], op=ALU.mult)
                                ins.then_inc(sems["V"], 1)
                                sch.inc("V", f"hm_{g}_{t}")
                        vector.drain()
                        if g >= 1 and pq == 0:
                            vector.wait_ge(sems["P"], sch.val("P", f"dnm_{g - 1}_0_1"))
                            vector.wait_ge(sems["P"], sch.val("P", f"dnm_{g - 1}_1_1"))
                        f0 = sp0 * 512
                        dstap = hq_t[:].rearrange("p (c4 f) -> p c4 f", c4=2)[:, :, f0:f0 + w0 + w1]
                        ins = q_vec(vector, q_t[:, 0:2 * (w0 + w1)], dstap, 2 * (w0 + w1), f"hq_{g}_{pq}")
                        ins.then_inc(sems["V"], 1)
                        sch.inc("V", f"hq_{g}_{pq}")
                    for nh in range(2):
                        for c4 in range(2):
                            cgl = 2 * g + c4
                            vector.wait_ge(sems["P"], sch.val("P", f"dnm_{g}_{c4}_{nh}"))
                            if nh == 0 and g >= 1:
                                dosem = "Do%d" % c4
                                vector.wait_ge(sems[dosem], sch.val(dosem, f"st_{2 * (g - 1) + c4}"))
                            ins = vector.tensor_tensor(
                                out=outb[:, c4 * D + nh * 512:c4 * D + nh * 512 + 512],
                                in0=x1[:, (cgl % 2) * D + nh * 512:(cgl % 2) * D + nh * 512 + 512],
                                in1=(p_fa if c4 == 0 else p_fb)[:, 0:512], op=ALU.add)
                            ins.then_inc(sems["V"], 1)
                            sch.inc("V", f"od_{g}_{c4}_{nh}")
                wq_units_vec(vector, cc)
                prev_own = tag

        def gp_fn(gpsimd):
            ktm_v = ktm[:].rearrange("p (bs j) -> p bs j", j=32)[:, :, 0:16]
            qtm_v = qtm[:].rearrange("p (bs j) -> p bs j", j=32)[:, :, 0:16]
            kf_h = kf[:].rearrange("p (h d) -> p h d", h=16)
            q2_h = q2t[:].rearrange("p (h d) -> p h d", h=8)
            prev_own = None
            prev = None
            for ci, cc in enumerate(ALLC):
                ph, c = cc
                tag = TAG[cc]
                gpsimd.wait_ge(sems["V"], sch.val("V", f"lv1_{tag}"))
                if prev is not None:
                    gpsimd.wait_ge(sems["S"], sch.val("S", f"ln1_{prev}"))
                gpsimd.tensor_tensor(out=rstd1[:], in0=veps1[:], in1=nhalf_tile(), op=ALU.pow)
                gpsimd.drain().then_inc(sems["G"], 1)
                sch.inc("G", f"rs1_{tag}")
                if ph == "r":
                    gpsimd.wait_ge(sems["S"], sch.val("S", f"ktm_{tag}"))
                    if prev is not None and prev.startswith("r"):
                        gpsimd.wait_ge(sems["P"], sch.val("P", f"supq_{prev}_15"))
                    gpsimd.tensor_scalar(out=k32[:], in0=ktm[:], scalar1=1.0 / 32, scalar2=None, op0=ALU.mult)
                    gpsimd.drain()
                    gpsimd.tensor_scalar(out=kf_h[:, :, 0:16], in0=ktm_v, scalar1=0.25, scalar2=None, op0=ALU.mult)
                    k32_v = k32[:].rearrange("p (bs j) -> p bs j", j=32)
                    for i in range(16):
                        c1 = k32_v[:, :, i:i + 1].to_broadcast([128, 16, 16])
                        gpsimd.tensor_tensor(out=kf_h[:, :, 17 + 16 * i:33 + 16 * i], in0=ktm_v, in1=c1, op=ALU.mult)
                    gpsimd.drain().then_inc(sems["G"], 1)
                    sch.inc("G", f"kf_{tag}")
                    prev = tag
                    continue
                for p2 in range(8):
                    h0, h1 = 2 * p2, 2 * p2 + 1
                    gpsimd.wait_ge(sems["S"], sch.val("S", f"psq_{tag}_{h1}"))
                    if h0 < 8:
                        if prev_own is not None:
                            gpsimd.wait_ge(sems["P"], sch.val("P", f"ymm_{prev_own}_{(h1 + 8) // 4}"))
                    else:
                        gpsimd.wait_ge(sems["P"], sch.val("P", f"ymm_{tag}_{(h1 - 8) // 4}"))
                    gpsimd.tensor_scalar(out=psc[:], in0=psc[:], scalar1=1.0 / 32, scalar2=0.5,
                                         op0=ALU.mult, op1=ALU.add)
                    gpsimd.drain()
                    for h in (h0, h1):
                        gpsimd.affine_select(out=att[:, (h % 8) * 128:(h % 8) * 128 + 128],
                                             in_=psc[:, (h % 2) * 128:(h % 2) * 128 + 128],
                                             pattern=[[1, 128]], compare_op=ALU.is_ge, fill=0.0,
                                             base=0, channel_multiplier=-1)
                    gpsimd.drain().then_inc(sems["G"], 1)
                    sch.inc("G", f"attp_{tag}_{p2}")
                gpsimd.wait_ge(sems["V"], sch.val("V", f"lv2_{tag}"))
                if prev_own is not None:
                    gpsimd.wait_ge(sems["S"], sch.val("S", f"ln2_{prev_own}"))
                gpsimd.tensor_tensor(out=rstd2[:], in0=veps2[:], in1=nhalf_tile(), op=ALU.pow)
                gpsimd.drain().then_inc(sems["G"], 1)
                sch.inc("G", f"rs2_{tag}")
                prev = tag
                prev_own = tag

        nhalf = S_("nhalf", [128, 1], F32)

        def nhalf_tile():
            return nhalf[:]

        def wq_spill_evacs_unit(scalar, cc, mat, r0, nr):
            if True:
                W = MATD[mat][2]
                ncg = W // 128
                nbat = (ncg + 3) // 4
                for bt in range(nbat):
                    ncg_b = min(4, ncg - bt * 4)
                    scalar.wait_ge(sems["P"], sch.val("P", f"wqt_{mat}{r0}_b{bt}"))
                    if bt >= 1:
                        pn = min(4, ncg - (bt - 1) * 4)
                        scalar.wait_ge(sems["Ws"], sch.val("Ws", f"wqd_{mat}{r0}_b{bt - 1}_{pn - 1}"))
                    elif WQ_PREV[(mat, r0)] is not None:
                        pmat, pr0, pnr = WQ_PREV[(mat, r0)]
                        pW = MATD[pmat][2]
                        pncg = pW // 128
                        pnbat = (pncg + 3) // 4
                        pn = min(4, pncg - (pnbat - 1) * 4)
                        scalar.wait_ge(sems["Ws"], sch.val("Ws", f"wqd_{pmat}{pr0}_b{pnbat - 1}_{pn - 1}"))
                    ins = scalar.copy(out=wsp[:, 0:ncg_b * nr * 128], in_=p_tr[:, 0:ncg_b * nr * 128])
                    ins.then_inc(sems["S"], 1)
                    sch.inc("S", f"wsp_{mat}{r0}_b{bt}")
                    scalar.wait_ge(sems["S"], sch.val("S", f"wsp_{mat}{r0}_b{bt}"))
                    ddst = MATD[mat][1]
                    for cgi in range(ncg_b):
                        cg = bt * 4 + cgi
                        scalar.dma_start(out=ddst[cg * 128:(cg + 1) * 128, r0 * 128:(r0 + nr) * 128],
                                         in_=wsp[:, cgi * nr * 128:(cgi + 1) * nr * 128]).then_inc(sems["Ws"], 16)
                        sch.inc("Ws", f"wqd_{mat}{r0}_b{bt}_{cgi}", 16)

        def sc_fn(scalar):
            prev = None
            prev_own = None
            for ci, cc in enumerate(ALLC):
                ph, c = cc
                tag = TAG[cc]
                xs = xt[:, (ci % 2) * D:(ci % 2 + 1) * D]
                if ph == "r":
                    wq_units_sc(scalar, cc, ladders=WQ_LADDERS.get(cc, [])[:1], qsc_only=True)
                scalar.wait_ge(sems["V"], sch.val("V", f"nb1_{tag}"))
                if prev is not None:
                    scalar.wait_ge(sems["P"], sch.val("P", f"xnt_{prev}"))
                ins = scalar.activation(xn1[:], xs, AF.Identity, scale=rstd1[:], bias=nbias1[:])
                ins.then_inc(sems["S"], 1)
                sch.inc("S", f"ln1_{tag}")
                scalar.wait_ge(sems["P"], sch.val("P", f"xnt_{tag}"))
                wlab = f"qproj_{prev}" if (prev is not None and prev.startswith("o")) else (f"vproj_{prev}_1" if prev is not None else None)
                if wlab is not None:
                    scalar.wait_ge(sems["P"], sch.val("P", wlab))
                ins = scalar.copy(out=xnf[:], in_=p_tr[:, 0:1024])
                ins.then_inc(sems["S"], 1)
                sch.inc("S", f"xnf_{tag}")
                scalar.wait_ge(sems["P"], sch.val("P", f"kproj_{tag}"))
                if prev is not None:
                    lab = f"sc_{prev}_15" if prev.startswith("o") else f"ktr_{prev}"
                    scalar.wait_ge(sems["P"], sch.val("P", lab))
                ins = scalar.copy(out=kfp[:], in_=p_pb[:, 0:512])
                ins.then_inc(sems["S"], 1)
                sch.inc("S", f"kfp_{tag}")
                for half in range(2):
                    scalar.wait_ge(sems["P"], sch.val("P", f"vproj_{tag}_{half}"))
                    if prev is not None and half == 0:
                        scalar.wait_ge(sems["P"], sch.val("P", f"supq_{prev}_15"))
                    dst = vga[:].rearrange("p (h e) -> p h e", h=16)[:, half * 8:(half + 1) * 8, 0:64]
                    ins = scalar.copy(out=dst, in_=(p_pa if half == 0 else p_pb)[:, 0:512])
                    ins.then_inc(sems["S"], 1)
                    sch.inc("S", f"vga_{tag}_{half}")
                scalar.wait_ge(sems["P"], sch.val("P", f"ktr_{tag}"))
                if prev is not None:
                    ksem = "G" if prev.startswith("r") else "V"
                    scalar.wait_ge(sems[ksem], sch.val(ksem, f"kf_{prev}"))
                ins = scalar.copy(out=ktm[:], in_=p_tr[:, 0:512])
                ins.then_inc(sems["S"], 1)
                sch.inc("S", f"ktm_{tag}")
                if ph == "r":
                    wq_units_sc(scalar, cc, skip_first_qsc=True)
                    if ci == n_rec - 1:
                        scalar.wait_ge(sems["V"], sch.val("V", "gated"))
                        scalar.copy(out=sb1b[:], in_=sb1[:])
                        ins = scalar.copy(out=sqb[:], in_=sq_[:])
                        ins.then_inc(sems["S"], 1)
                        sch.inc("S", "mir_boot")
                    prev = tag
                    continue
                scalar.wait_ge(sems["P"], sch.val("P", f"qproj_{tag}"))
                if prev_own is not None:
                    scalar.wait_ge(sems["P"], sch.val("P", f"ymm_{prev_own}_3"))
                ins = scalar.activation(qfp[:], p_pa[:, 0:512], AF.Identity, bias=onehot[:], scale=1.0)
                ins.then_inc(sems["S"], 1)
                sch.inc("S", f"qfp_{tag}")
                scalar.wait_ge(sems["P"], sch.val("P", f"qtr_{tag}"))
                if prev_own is not None:
                    scalar.wait_ge(sems["V"], sch.val("V", f"q2b_{prev_own}_1"))
                ins = scalar.copy(out=qtm[:], in_=p_tr[:, 512:1024])
                ins.then_inc(sems["S"], 1)
                sch.inc("S", f"qtm_{tag}")
                for half2 in range(2):
                    for gq in (2 * half2, 2 * half2 + 1):
                        scalar.wait_ge(sems["P"], sch.val("P", f"q2t_{tag}_{gq}"))
                        if gq >= 2:
                            scalar.wait_ge(sems["P"], sch.val("P", f"ymm_{tag}_{gq - 2}"))
                        elif prev_own is not None:
                            scalar.wait_ge(sems["P"], sch.val("P", f"ymm_{prev_own}_{gq + 2}"))
                        ins = scalar.copy(out=q2f[:, (gq % 2) * 1024:(gq % 2 + 1) * 1024], in_=p_tr[:, 0:1024])
                        ins.then_inc(sems["S"], 1)
                        sch.inc("S", f"q2f_{tag}_{gq}")
                    for h in range(8 * half2, 8 * half2 + 8):
                        scalar.wait_ge(sems["P"], sch.val("P", f"sc_{tag}_{h}"))
                        if h >= 2:
                            scalar.wait_ge(sems["G"], sch.val("G", f"attp_{tag}_{(h - 2) // 2}"))
                        elif prev_own is not None:
                            scalar.wait_ge(sems["G"], sch.val("G", f"attp_{prev_own}_7"))
                        ins = scalar.activation(psc[:, (h % 2) * 128:(h % 2) * 128 + 128],
                                                SCB(h),
                                                AF.Square, bias=four[:], scale=1.0)
                        ins.then_inc(sems["S"], 1)
                        sch.inc("S", f"psq_{tag}_{h}")
                    for grp in (2 * half2, 2 * half2 + 1):
                        scalar.wait_ge(sems["P"], sch.val("P", f"ymm_{tag}_{grp}"))
                        if grp >= 2:
                            scalar.wait_ge(sems["V"], sch.val("V", f"yd_{tag}_{(grp - 2) * 4 + 3}"))
                        elif prev_own is not None:
                            scalar.wait_ge(sems["V"], sch.val("V", f"yd_{prev_own}_{(grp + 2) * 4 + 3}"))
                        ins = scalar.copy(out=ysb[0:65, (grp % 2) * 512:(grp % 2) * 512 + 512],
                                          in_=YBANK(grp))
                        ins.then_inc(sems["S"], 1)
                        sch.inc("S", f"yev_{tag}_{grp}")
                scalar.wait_ge(sems["V"], sch.val("V", f"adq_{tag}_15"))
                scalar.copy(out=sb1b[:], in_=sb1[:])
                ins = scalar.copy(out=sqb[:], in_=sq_[:])
                ins.then_inc(sems["S"], 1)
                sch.inc("S", f"mir_{tag}")
                scalar.wait_ge(sems["V"], sch.val("V", f"nb2_{tag}"))
                if c >= 1:
                    scalar.wait_ge(sems["V"], sch.val("V", f"xq_o{c - 1}"))
                xv = x1[:, (c % 2) * D:(c % 2 + 1) * D]
                ins = scalar.activation(xn2[:], xv, AF.Identity, scale=rstd2[:], bias=nbias2[:])
                ins.then_inc(sems["S"], 1)
                sch.inc("S", f"ln2_{tag}")
                q_sc(scalar, f"xq_{tag}", D, xq2[:, (c % 2) * D:(c % 2 + 1) * D])
                scalar.wait_ge(sems["P"], sch.val("P", f"xqt_{tag}"))
                ins = scalar.copy(out=xq2[:, (c % 2) * D:(c % 2 + 1) * D], in_=p_tr[:, 0:1024])
                ins.then_inc(sems["S"], 1)
                sch.inc("S", f"xqf_{tag}")
                g, c2 = c // 2, c % 2
                if c2 == 1:
                    for pq in range(3):
                        sp0, sp1 = 2 * pq, 2 * pq + 1
                        w0, w1 = SW(sp0), SW(sp1)
                        for sp in (sp0, sp1):
                            for c4 in range(2):
                                t = sp * 2 + c4
                                scalar.wait_ge(sems["P"], sch.val("P", f"gt_{g}_{t}"))
                                if t >= 1:
                                    scalar.wait_ge(sems["V"], sch.val("V", f"hm_{g}_{t - 1}"))
                                elif g >= 1:
                                    scalar.wait_ge(sems["V"], sch.val("V", f"hm_{g - 1}_11"))
                                sw = SW(sp)
                                ins = scalar.activation(gsil[:, 0:sw], p_fa[:, 0:sw], AF.Silu)
                                ins.then_inc(sems["S"], 1)
                                sch.inc("S", f"sil_{g}_{t}")
                        hdst = hq_t[:].rearrange("p (c4 f) -> p c4 f", c4=2)[:, :, sp0 * 512:sp0 * 512 + w0 + w1]
                        q_sc(scalar, f"hq_{g}_{pq}", 2 * (w0 + w1), hdst)
                    for c4 in range(2):
                        for bq in range(3):
                            w = 1024 if bq < 2 else 768
                            scalar.wait_ge(sems["P"], sch.val("P", f"hqt_{g}_{c4}_{bq}"))
                            ins = scalar.copy(out=hq_t[:, c4 * IF + bq * 1024:c4 * IF + bq * 1024 + w],
                                              in_=p_tr[:, 0:w])
                            ins.then_inc(sems["S"], 1)
                            sch.inc("S", f"hqf_{g}_{c4}_{bq}")
                wq_units_sc(scalar, cc)
                prev = tag
                prev_own = tag

        def SCB(h):
            return (p_pa if h % 2 == 0 else p_pb)[:, 0:128]

        def YSL(h):
            bk = p_y if (h // 4) % 2 == 0 else p_sc
            return bk[0:65, (h % 4) * 128:(h % 4) * 128 + 128]

        def YBANK(grp):
            bk = p_y if grp % 2 == 0 else p_sc
            return bk[0:65, 0:512]

        def RSL(h):
            return (p_pa if h % 2 == 0 else p_pb)[0:64, 128:256]

        def DBL(grp):
            return p_ds[:, 0:65] if grp % 2 == 0 else p_sc[:, 256:321]

        def DQL(h):
            return p_ds[:, 65:195] if h % 2 == 0 else p_sc[:, 321:451]

        def last_tr_label(cc):
            ph, c = cc
            tag = TAG[cc]
            units = WQ_AT.get(cc, [])
            if units:
                mat, r0, nr = units[-1]
                W = MATD[mat][2]
                nbat = (W // 128 + 3) // 4
                return f"wsp_{mat}{r0}_b{nbat - 1}"
            if ph == "o" and c % 2 == 1:
                return f"hqf_{c // 2}_1_2"
            if ph == "o":
                return f"xqf_{tag}"
            return f"ktm_{tag}"

        def pre_wq_tr_label(cc):
            ph, c = cc
            tag = TAG[cc]
            if ph == "o" and c % 2 == 1:
                return f"hqf_{c // 2}_1_2"
            if ph == "o":
                return f"xqf_{tag}"
            return f"ktm_{tag}"

        def pinc(ins, label):
            ins.then_inc(sems["P"], 1)
            sch.inc("P", label)

        def pe_fn(tensor):
            tensor.wait_ge(sems["V"], sch.val("V", "wc_" + WLOADS[-1][0]))
            prev = None
            prev_own = None
            for ci, cc in enumerate(ALLC):
                ph, c = cc
                tag = TAG[cc]
                tensor.wait_ge(sems["S"], sch.val("S", f"ln1_{tag}"))
                if ci >= 1:
                    tensor.wait_ge(sems["S"], sch.val("S", last_tr_label(ALLC[ci - 1])))
                for t8 in range(8):
                    ins = tensor.transpose(p_tr[:, t8 * 128:(t8 + 1) * 128], xn1[:, t8 * 128:(t8 + 1) * 128], identb[:])
                pinc(ins, f"xnt_{tag}")
                tensor.wait_ge(sems["S"], sch.val("S", f"xnf_{tag}"))
                if prev is not None:
                    lab = f"x1b_{prev}" if prev.startswith("o") else None
                    if lab:
                        tensor.wait_ge(sems["V"], sch.val("V", lab))
                    else:
                        tensor.wait_ge(sems["S"], sch.val("S", f"vga_{prev}_1"))
                for b in range(4):
                    for k in range(8):
                        ins = tensor.matmul(p_pb[:, b * 128:(b + 1) * 128],
                                            lhsT=wk_s[:, k * 512 + b * 128:k * 512 + (b + 1) * 128],
                                            rhs=xnf[:, k * 128:(k + 1) * 128], start=(k == 0), stop=(k == 7))
                pinc(ins, f"kproj_{tag}")
                tensor.wait_ge(sems["S"], sch.val("S", f"kfp_{tag}"))
                if prev is not None:
                    lab = f"x1_{prev}_0" if prev.startswith("o") else None
                    if lab:
                        tensor.wait_ge(sems["V"], sch.val("V", lab))
                    else:
                        tensor.wait_ge(sems["S"], sch.val("S", f"vga_{prev}_0"))
                for half in range(2):
                    bank = p_pa if half == 0 else p_pb
                    for k in range(8):
                        ins = tensor.matmul(bank[:, 0:512], lhsT=xnf[:, k * 128:(k + 1) * 128],
                                            rhs=wv_s[:, k * 1024 + half * 512:k * 1024 + half * 512 + 512],
                                            start=(k == 0), stop=(k == 7))
                    pinc(ins, f"vproj_{tag}_{half}")
                tensor.wait_ge(sems["S"], sch.val("S", f"xnf_{tag}"))
                for b in range(4):
                    ins = tensor.transpose(p_tr[:, b * 128:(b + 1) * 128], kfp[:, b * 128:(b + 1) * 128], identb[:])
                pinc(ins, f"ktr_{tag}")

                def sup():
                    ksem = "G" if ph == "r" else "V"
                    tensor.wait_ge(sems[ksem], sch.val(ksem, f"kf_{tag}"))
                    tensor.wait_ge(sems["S"], sch.val("S", f"vga_{tag}_1"))
                    if ph == "o":
                        tensor.wait_ge(sems["V"], sch.val("V", f"yd_{tag}_15"))
                    for grp in range(4):
                        if grp >= 2:
                            tensor.wait_ge(sems["V"], sch.val("V", f"adb_{tag}_{grp - 2}"))
                        elif prev is not None:
                            tensor.wait_ge(sems["V"], sch.val("V", f"adb_{prev}_{grp + 2}"))
                        dbl = DBL(grp)
                        for s4 in range(4):
                            h = grp * 4 + s4
                            ins = tensor.matmul(dbl[32 * s4:32 * s4 + 17, 0:65],
                                                lhsT=kf[:, h * 273:h * 273 + 17],
                                                rhs=vga[:, h * 65:(h + 1) * 65], start=True, stop=True,
                                                tile_position=(0, 32 * s4))
                        pinc(ins, f"supb_{tag}_{grp}")
                    for h in range(16):
                        if h >= 2:
                            tensor.wait_ge(sems["V"], sch.val("V", f"adq_{tag}_{h - 2}"))
                        else:
                            tensor.wait_ge(sems["V"], sch.val("V", f"adb_{tag}_{h + 2}"))
                            if prev is not None:
                                tensor.wait_ge(sems["V"], sch.val("V", f"adq_{prev}_{h + 14}"))
                        dql = DQL(h)
                        for t2 in range(2):
                            ins = tensor.matmul(dql[:, t2 * 65:(t2 + 1) * 65],
                                                lhsT=kf[:, h * 273 + 17 + t2 * 128:h * 273 + 17 + (t2 + 1) * 128],
                                                rhs=vga[:, h * 65:(h + 1) * 65], start=True, stop=True)
                        pinc(ins, f"supq_{tag}_{h}")

                if ph == "r":
                    sup()
                    wq_transposes(tensor, cc)
                    prev = tag
                    continue
                tensor.wait_ge(sems["S"], sch.val("S", f"vga_{tag}_0"))
                for b in range(4):
                    for k in range(8):
                        ins = tensor.matmul(p_pa[:, b * 128:(b + 1) * 128],
                                            lhsT=wq_s[:, k * 512 + b * 128:k * 512 + (b + 1) * 128],
                                            rhs=xnf[:, k * 128:(k + 1) * 128], start=(k == 0), stop=(k == 7))
                pinc(ins, f"qproj_{tag}")
                tensor.wait_ge(sems["S"], sch.val("S", f"qfp_{tag}"))
                tensor.wait_ge(sems["S"], sch.val("S", f"ktm_{tag}"))
                for b in range(4):
                    ins = tensor.transpose(p_tr[:, 512 + b * 128:512 + (b + 1) * 128], qfp[:, b * 128:(b + 1) * 128], identb[:])
                pinc(ins, f"qtr_{tag}")
                if prev_own is None:
                    tensor.wait_ge(sems["S"], sch.val("S", "mir_boot"))
                else:
                    tensor.wait_ge(sems["S"], sch.val("S", f"mir_{prev_own}"))
                    tensor.wait_ge(sems["V"], sch.val("V", f"adq_{prev_own}_15"))
                for half2 in range(2):
                    for gq in (2 * half2, 2 * half2 + 1):
                        tensor.wait_ge(sems["V"], sch.val("V", f"q2b_{tag}_{gq // 2}"))
                        if gq >= 1:
                            tensor.wait_ge(sems["S"], sch.val("S", f"q2f_{tag}_{gq - 1}"))
                        else:
                            tensor.wait_ge(sems["S"], sch.val("S", f"qtm_{tag}"))
                        for t8 in range(8):
                            idx = (gq % 2) * 8 + t8
                            ins = tensor.transpose(p_tr[:, t8 * 128:(t8 + 1) * 128],
                                                   q2t[:, idx * 128:(idx + 1) * 128], identb[:])
                        pinc(ins, f"q2t_{tag}_{gq}")
                    for h in range(8 * half2, 8 * half2 + 8):
                        b, s = h // 4, h % 4
                        if h == 0:
                            tensor.wait_ge(sems["S"], sch.val("S", f"qfp_{tag}"))
                        if h in (8, 9):
                            tensor.wait_ge(sems["V"], sch.val("V", f"yd_{tag}_{6 + (h % 2)}"))
                        if h >= 2:
                            tensor.wait_ge(sems["S"], sch.val("S", f"psq_{tag}_{h - 2}"))
                        elif prev_own is not None:
                            tensor.wait_ge(sems["S"], sch.val("S", f"psq_{prev_own}_{h + 14}"))
                        ins = tensor.matmul(SCB(h),
                                            lhsT=kfp[32 * s:32 * s + 16, b * 128:(b + 1) * 128],
                                            rhs=qfp[32 * s:32 * s + 16, b * 128:(b + 1) * 128], start=True, stop=True,
                                            tile_position=(32 * s, 0))
                        pinc(ins, f"sc_{tag}_{h}")
                    for grp in (2 * half2, 2 * half2 + 1):
                        if prev_own is not None:
                            tensor.wait_ge(sems["S"], sch.val("S", f"yev_{prev_own}_{grp}"))
                        tensor.wait_ge(sems["S"], sch.val("S", f"q2f_{tag}_{grp}"))
                        for s4 in range(4):
                            h = grp * 4 + s4
                            b, s = h // 4, h % 4
                            tensor.wait_ge(sems["G"], sch.val("G", f"attp_{tag}_{h // 2}"))
                            yslot = YSL(h)
                            tensor.matmul(yslot, lhsT=vga[:, h * 65:(h + 1) * 65],
                                          rhs=att[:, (h % 8) * 128:(h % 8) * 128 + 128], start=True, stop=False)
                            tensor.matmul(yslot, lhsT=sb1b[32 * s:32 * s + 17, b * 65:(b + 1) * 65],
                                          rhs=qfp[32 * s:32 * s + 17, b * 128:(b + 1) * 128], start=False, stop=False,
                                          tile_position=(32 * s, 0))
                            for t2 in range(2):
                                ins = tensor.matmul(yslot, lhsT=sqb[:, h * 130 + t2 * 65:h * 130 + (t2 + 1) * 65],
                                                    rhs=q2f[:, (grp % 2) * 1024 + (s4 * 2 + t2) * 128:(grp % 2) * 1024 + (s4 * 2 + t2 + 1) * 128],
                                                    start=False, stop=(t2 == 1))
                        pinc(ins, f"ymm_{tag}_{grp}")
                    for h in range(8 * half2, 8 * half2 + 8):
                        grp = h // 4
                        if h % 8 == 0:
                            tensor.wait_ge(sems["S"], sch.val("S", f"psq_{tag}_{8 * half2 + 7}"))
                        tensor.wait_ge(sems["V"], sch.val("V", f"rd_{tag}_{grp}"))
                        if h >= 2:
                            tensor.wait_ge(sems["V"], sch.val("V", f"yd_{tag}_{h - 2}"))
                        elif prev_own is not None:
                            tensor.wait_ge(sems["V"], sch.val("V", f"yd_{prev_own}_{h + 14}"))
                        ins = tensor.matmul(RSL(h),
                                            lhsT=ones64[0:1, 0:64],
                                            rhs=rden[0:1, (grp % 2) * 512 + (h % 4) * 128:(grp % 2) * 512 + (h % 4) * 128 + 128],
                                            start=True, stop=True)
                        pinc(ins, f"rep_{tag}_{h}")
                sup()
                tensor.wait_ge(sems["V"], sch.val("V", f"yd_{tag}_15"))
                for half in range(2):
                    bank = p_pa if half == 0 else p_pb
                    for k in range(8):
                        ins = tensor.matmul(bank[:, 0:512], lhsT=yf[:, k * 128:(k + 1) * 128],
                                            rhs=wo_s[:, k * 1024 + half * 512:k * 1024 + half * 512 + 512],
                                            start=(k == 0), stop=(k == 7))
                    pinc(ins, f"ao_{tag}_{half}")
                tensor.wait_ge(sems["V"], sch.val("V", f"xq_{tag}"))
                tensor.wait_ge(sems["S"], sch.val("S", f"q2f_{tag}_3"))
                for t8 in range(8):
                    ins = tensor.transpose(p_tr[:, t8 * 128:(t8 + 1) * 128],
                                           xq2[:, (c % 2) * D + t8 * 128:(c % 2) * D + (t8 + 1) * 128], identb[:])
                pinc(ins, f"xqt_{tag}")
                g, c2 = c // 2, c % 2
                if c2 == 1:
                    for sp in range(6):
                        sw = SW(sp)
                        for c4 in range(2):
                            t = sp * 2 + c4
                            if c4 == 0:
                                tensor.wait_ge(sems["Wg00"], sch.val("Wg00", f"gu_{g}_{sp}_0_0"))
                                tensor.wait_ge(sems["Wg10"], sch.val("Wg10", f"gu_{g}_{sp}_0_1"))
                                if sp >= 1:
                                    tensor.wait_ge(sems["S"], sch.val("S", f"sil_{g}_{2 * sp - 1}"))
                                elif g >= 1:
                                    tensor.wait_ge(sems["S"], sch.val("S", f"sil_{g - 1}_11"))
                                    tensor.wait_ge(sems["V"], sch.val("V", f"od_{g - 1}_0_1"))
                                    tensor.wait_ge(sems["V"], sch.val("V", f"od_{g - 1}_1_1"))
                            else:
                                tensor.wait_ge(sems["S"], sch.val("S", f"sil_{g}_{2 * sp}"))
                            tensor.wait_ge(sems["S"], sch.val("S", f"xqf_o{2 * g + c4}"))
                            for k in range(8):
                                hk = k // 4
                                ins = tensor.matmul(p_fa[:, 0:sw],
                                                    lhsT=xq2[:, c4 * D + k * 128:c4 * D + (k + 1) * 128],
                                                    rhs=gu_buf[:, hk * 2048 + (k % 4) * sw:hk * 2048 + (k % 4) * sw + sw],
                                                    start=(k == 0), stop=(k == 7))
                            pinc(ins, f"gt_{g}_{t}")
                        for c4 in range(2):
                            t = sp * 2 + c4
                            if c4 == 0:
                                tensor.wait_ge(sems["Wg01"], sch.val("Wg01", f"gu_{g}_{sp}_1_0"))
                                tensor.wait_ge(sems["Wg11"], sch.val("Wg11", f"gu_{g}_{sp}_1_1"))
                                if sp >= 1:
                                    tensor.wait_ge(sems["V"], sch.val("V", f"hm_{g}_{2 * sp - 1}"))
                                elif g >= 1:
                                    tensor.wait_ge(sems["V"], sch.val("V", f"hq_{g - 1}_2"))
                            else:
                                tensor.wait_ge(sems["V"], sch.val("V", f"hm_{g}_{2 * sp}"))
                            for k in range(8):
                                hk = k // 4
                                ins = tensor.matmul(p_fb[:, 0:sw],
                                                    lhsT=xq2[:, c4 * D + k * 128:c4 * D + (k + 1) * 128],
                                                    rhs=gu_buf[:, hk * 2048 + (k % 4) * sw:hk * 2048 + (k % 4) * sw + sw],
                                                    start=(k == 0), stop=(k == 7))
                            pinc(ins, f"up_{g}_{sp}_{c4}")
                    for c4 in range(2):
                        for bq in range(3):
                            nt = 8 if bq < 2 else 6
                            tensor.wait_ge(sems["V"], sch.val("V", f"hq_{g}_{bq}"))
                            if bq >= 1:
                                tensor.wait_ge(sems["S"], sch.val("S", f"hqf_{g}_{c4}_{bq - 1}"))
                            elif c4 == 0:
                                tensor.wait_ge(sems["S"], sch.val("S", f"xqf_{tag}"))
                            else:
                                tensor.wait_ge(sems["S"], sch.val("S", f"hqf_{g}_0_2"))
                            for t8 in range(nt):
                                idx = bq * 8 + t8
                                ins = tensor.transpose(p_tr[:, t8 * 128:(t8 + 1) * 128],
                                                       hq_t[:, c4 * IF + idx * 128:c4 * IF + (idx + 1) * 128], identb[:])
                            pinc(ins, f"hqt_{g}_{c4}_{bq}")
                    for nh in range(2):
                        for kb in range(2):
                            tensor.wait_ge(sems["Wd"], sch.val("Wd", f"wd_{g}_{nh}_{kb}"))
                            for c4 in range(2):
                                bank = p_fa if c4 == 0 else p_fb
                                if nh == 0 and kb == 0:
                                    tensor.wait_ge(sems["V"], sch.val("V", f"hq_{g}_2"))
                                    tensor.wait_ge(sems["S"], sch.val("S", f"hqf_{g}_{c4}_2"))
                                if nh == 1 and kb == 0:
                                    tensor.wait_ge(sems["V"], sch.val("V", f"od_{g}_{c4}_0"))
                                for j in range(11):
                                    kk = kb * 11 + j
                                    ins = tensor.matmul(bank[:, 0:512],
                                                        lhsT=hq_t[:, c4 * IF + kk * 128:c4 * IF + (kk + 1) * 128],
                                                        rhs=wd_buf[:, j * 512:(j + 1) * 512],
                                                        start=(kb == 0 and j == 0), stop=(kb == 1 and j == 10))
                                if kb == 1:
                                    pinc(ins, f"dnm_{g}_{c4}_{nh}")
                            if kb == 0:
                                pinc(ins, f"dnb_{g}_{nh}_{kb}")
                            else:
                                dr = tensor.drain(fusable=True)
                                dr.then_inc(sems["P"], 1)
                                sch.inc("P", f"dnb_{g}_{nh}_{kb}")
                wq_transposes(tensor, cc)
                prev = tag
                prev_own = tag

        def wq_transposes(tensor, cc):
            prev_sp = pre_wq_tr_label(cc)
            for (mat, r0, nr) in WQ_AT.get(cc, []):
                W = MATD[mat][2]
                ncg = W // 128
                nbat = (ncg + 3) // 4
                for bt in range(nbat):
                    cg0 = bt * 4
                    ncg_b = min(4, ncg - cg0)
                    tensor.wait_ge(sems["V"], sch.val("V", f"wqq_{WQ_UNIT_LAST[(mat, r0)]}"))
                    if bt >= 1:
                        tensor.wait_ge(sems["S"], sch.val("S", f"wsp_{mat}{r0}_b{bt - 1}"))
                    else:
                        tensor.wait_ge(sems["S"], sch.val("S", prev_sp))
                    prev_sp = f"wsp_{mat}{r0}_b{nbat - 1}" if bt == nbat - 1 else prev_sp
                    for cgi in range(ncg_b):
                        cg = cg0 + cgi
                        off = WQ_DSTOFF.get((mat, r0), 0)
                        for j in range(nr):
                            ins = tensor.transpose(p_tr[:, (cgi * nr + j) * 128:(cgi * nr + j + 1) * 128],
                                                   wqn[:, off + j * W + cg * 128:off + j * W + (cg + 1) * 128],
                                                   identb[:])
                    pinc(ins, f"wqt_{mat}{r0}_b{bt}")

        fe = FakeEng()
        sch.record = True
        for fn in (sync_fn, vec_fn, gp_fn, sc_fn, pe_fn):
            sch.counts = dict(sch.counts)
            fn(fe)
        sch.record = False
        block = es.enter_context(nc.Block())

        @block.sync
        def _(sync):
            sync_fn(sync)

        @block.vector
        def _(vector):
            vec_fn(vector)

        @block.gpsimd
        def _(gpsimd):
            gpsimd.memset(nhalf[:], -0.5)
            gpsimd.drain()
            gp_fn(gpsimd)

        @block.scalar
        def _(scalar):
            sc_fn(scalar)

        @block.tensor
        def _(tensor):
            pe_fn(tensor)

    _LAST_SCHED['labels'] = dict(sch.labels)
    return nc



def _prep_weights(Wq, Wk):
    WqTp = np.zeros((D, 512), np.float32)
    WkTp = np.zeros((D, 512), np.float32)
    for h in range(16):
        b, s = h // 4, h % 4
        WqTp[:, b * 128 + 32 * s:b * 128 + 32 * s + 16] = Wq[h * 16:(h + 1) * 16, :].T
        WkTp[:, b * 128 + 32 * s:b * 128 + 32 * s + 16] = Wk[h * 16:(h + 1) * 16, :].T
    return WqTp, WkTp


_CACHE = {}


def kernel(L0_attn_hidden_states, L0_attn_Wq, L0_attn_Wk, L0_attn_Wv, L0_attn_Wo,
           L0_ffn_w_gate, L0_ffn_w_up, L0_ffn_w_down):
    x = np.ascontiguousarray(np.asarray(L0_attn_hidden_states, np.float32))
    Wq = np.asarray(L0_attn_Wq, np.float32)
    Wk = np.asarray(L0_attn_Wk, np.float32)
    Wv = np.asarray(L0_attn_Wv, np.float32)
    Wo = np.asarray(L0_attn_Wo, np.float32)
    wg = np.ascontiguousarray(np.asarray(L0_ffn_w_gate, np.float32))
    wu = np.ascontiguousarray(np.asarray(L0_ffn_w_up, np.float32))
    wd = np.ascontiguousarray(np.asarray(L0_ffn_w_down, np.float32))

    WqTp, WkTp = _prep_weights(Wq, Wk)
    WvT = np.ascontiguousarray(Wv.T)
    WoT = np.ascontiguousarray(Wo.T)
    ident = np.eye(128, dtype=np.float32)
    onehot = np.zeros((128, 1), np.float32)
    for s in range(4):
        onehot[32 * s + 16, 0] = 1.0

    if "nc" not in _CACHE:
        _CACHE["nc"] = build_program()
    nc = _CACHE["nc"]

    in_maps = []
    for core in range(8):
        b, half = core // 2, core % 2
        m = {
            "x_own": np.ascontiguousarray(x[b, half * 2048:(half + 1) * 2048]),
            "x_rec": np.ascontiguousarray(x[b, 0:2048]),
            "gatev": np.full((128, 1), float(half), np.float32),
            "onehot": onehot,
            "ident": ident,
            "WqTp": WqTp, "WkTp": WkTp, "WvT": WvT, "WoT": WoT,
            "w_gate": wg, "w_up": wu, "w_down": wd,
        }
        in_maps.append(m)
    res = run_bass_kernel_spmd(nc, in_maps, core_ids=list(range(8)))
    out = np.empty((4, 4096, D), np.float32)
    for core in range(8):
        b, half = core // 2, core % 2
        out[b, half * 2048:(half + 1) * 2048] = res.results[core]["out"]
    return out
